# version 01
# baseline (speedup 1.0000x reference)
import math
import numpy as np
import ml_dtypes
import concourse.bass as bass
import concourse.mybir as mybir
from concourse.bass_utils import run_bass_kernel_spmd

F32 = mybir.dt.float32
BF16 = mybir.dt.bfloat16
AF = mybir.ActivationFunctionType
ALU = mybir.AluOpType
AX = mybir.AxisListType

P = 128
D = 2048
DC = 16
NH = 16
E = 128
S = 4096
SOWN = 2048
DH = 8192
NSLOT = 16
SCW = 512
QSCALE = 1.0 / math.sqrt(128.0)

ENGS = ("sync", "scalar", "vector", "tensor", "gpsimd")
OCT_SLOTS = [[15, 13, 11, 9, 7, 5, 3, 1], [16, 14, 12, 10, 8, 6, 4, 2]]


def _slot_off(oct_slots):
    offs = {}
    o = 0
    for s in oct_slots:
        offs[s] = o
        o += 256 * s
    return offs


class Builder:
    def __init__(self):
        self.nc = bass.Bass("TRN2", target_bir_lowering=False)
        self.q = {e: [] for e in ENGS}
        self.cnt = {}
        self.sem = {}
        self.dma_sems = set()
        self.log = []

    def new_sem(self, name):
        self.sem[name] = self.nc.alloc_semaphore(name=name)
        self.cnt[name] = 0

    def emit(self, eng, f):
        self.q[eng].append(f)

    def dma(self, eng, dst, src, semname, **kw):
        if semname not in self.sem:
            self.new_sem(semname)
        self.dma_sems.add(semname)
        self.cnt[semname] += 16
        sem = self.sem[semname]

        def f(e, dst=dst, src=src, kw=kw, sem=sem):
            e.dma_start(dst, src, **kw).then_inc(sem, 16)
        self.emit(eng, f)
        self.log.append(("dma", eng, semname, self.cnt[semname], (), ()))
        return self.cnt[semname]

    def wait(self, eng, semname, val):
        if val is None or val <= 0:
            return
        assert val <= self.cnt[semname], (semname, val, self.cnt[semname])
        sem = self.sem[semname]

        def f(e, sem=sem, val=val):
            e.wait_ge(sem, val)
        self.emit(eng, f)
        self.log.append(("wait", eng, semname, val, (), ()))

    def wait_all_dma(self, eng):
        for s in sorted(self.dma_sems):
            self.wait(eng, s, self.cnt[s])

    ENGSEM = {"tensor": "s_pe", "scalar": "s_ac", "vector": "s_ve", "gpsimd": "s_po"}

    def op(self, eng, f, semname=None, pr=(), pw=()):
        self.log.append(("op", eng, semname,
                         self.cnt[semname] + 1 if semname else None, pr, pw))
        if semname is None:
            self.emit(eng, f)
            return None
        self.cnt[semname] += 1
        sem = self.sem[semname]

        def g(e, f=f, sem=sem):
            f(e).then_inc(sem, 1)
        self.emit(eng, g)
        return self.cnt[semname]


NPHASES = 4
PH1_MODE = "full"
PH3_PARTS = "full"
PH2_PARTS = "full"
PH2_RCUT = 9
PH2_NSC = 4
PH3_NDBLK = 4
PH3_NHTQ = 16
PH3_NSC = 4
PH3_FV = "act"
LAST_RES = None
TAPS = ()


def build_program():
    b = Builder()
    nc = b.nc

    x_kv = nc.dram_tensor("x_kv", [D, S], F32, kind="ExternalInput").ap()
    x_q = nc.dram_tensor("x_q", [D, SOWN], F32, kind="ExternalInput").ap()
    wq_t = nc.dram_tensor("wq_t", [NH * D, E], F32, kind="ExternalInput").ap()
    wk_t = nc.dram_tensor("wk_t", [NH * D, E], F32, kind="ExternalInput").ap()
    wv_t = nc.dram_tensor("wv_t", [NH * D, E], F32, kind="ExternalInput").ap()
    wo_t = nc.dram_tensor("wo_t", [NH * E, D], F32, kind="ExternalInput").ap()
    up_t = nc.dram_tensor("up_t", [D, DH], F32, kind="ExternalInput").ap()
    down_t = nc.dram_tensor("down_t", [DH, D], F32, kind="ExternalInput").ap()
    upsc2d = nc.dram_tensor("upsc2d", [64, P], F32, kind="ExternalInput").ap()
    dssc2d = nc.dram_tensor("dssc2d", [16, P], F32, kind="ExternalInput").ap()
    osc = nc.dram_tensor("osc", [1, 1], F32, kind="ExternalInput").ap()
    mask512 = nc.dram_tensor("mask512", [P, 512], F32, kind="ExternalInput").ap()
    eye_bf = nc.dram_tensor("eye_bf", [P, P], BF16, kind="ExternalInput").ap()
    eye_f32 = nc.dram_tensor("eye_f32", [P, P], F32, kind="ExternalInput").ap()
    ones_bf = nc.dram_tensor("ones_bf", [P, P], BF16, kind="ExternalInput").ap()
    ones_f32 = nc.dram_tensor("ones_f32", [P, P], F32, kind="ExternalInput").ap()
    nones_f32 = nc.dram_tensor("nones_f32", [P, P], F32, kind="ExternalInput").ap()
    outT = nc.dram_tensor("outT", [D, SOWN], F32, kind="ExternalOutput").ap()

    k_scr = nc.dram_tensor("k_scr", [NH * P, S], BF16, kind="Internal").ap()
    v_scr = nc.dram_tensor("v_scr", [NH * P, S], BF16, kind="Internal").ap()
    q_scr = nc.dram_tensor("q_scr", [NH * P, SOWN], BF16, kind="Internal").ap()
    z_scr = nc.dram_tensor("z_scr", [NH * P, SOWN], BF16, kind="Internal").ap()
    xm32_scr = nc.dram_tensor("xm32_scr", [DC * P, SOWN], F32, kind="Internal").ap()
    xmbf_scr = nc.dram_tensor("xmbf_scr", [DC * P, SOWN], BF16, kind="Internal").ap()
    up_scr = nc.dram_tensor("up_scr", [16 * D, 512], BF16, kind="Internal").ap()
    down_scr = nc.dram_tensor("down_scr", [4 * DH, 512], BF16, kind="Internal").ap()

    eyeb_sb = nc.alloc_sbuf_tensor("eyeb_sb", [P, P], BF16)
    eyef_sb = nc.alloc_sbuf_tensor("eyef_sb", [P, P], F32)
    onesb_sb = nc.alloc_sbuf_tensor("onesb_sb", [P, P], BF16)
    onesf_sb = nc.alloc_sbuf_tensor("onesf_sb", [P, P], F32)
    nonesf_sb = nc.alloc_sbuf_tensor("nonesf_sb", [P, P], F32)
    mask_sb = nc.alloc_sbuf_tensor("mask_sb", [P, 512], F32)
    upsc_sb = nc.alloc_sbuf_tensor("upsc_sb", [P, 64], F32)
    dseff_sb = nc.alloc_sbuf_tensor("dseff_sb", [P, 16], F32)
    scst_sb = nc.alloc_sbuf_tensor("scst_sb", [64, P], F32)
    osb_sb = nc.alloc_sbuf_tensor("osb_sb", [P, 1], F32)
    negmax_sb = nc.alloc_sbuf_tensor("negmax_sb", [P, 16], F32)
    sums_sb = nc.alloc_sbuf_tensor("sums_sb", [P, 16], F32)
    lnsum_sb = nc.alloc_sbuf_tensor("lnsum_sb", [P, 16], F32)
    recip_sb = nc.alloc_sbuf_tensor("recip_sb", [P, 16], F32)
    r_sum = nc.alloc_sbuf_tensor("r_sum", [1, 512], F32)
    r_sq = nc.alloc_sbuf_tensor("r_sq", [1, 512], F32)
    r_mean = nc.alloc_sbuf_tensor("r_mean", [1, 512], F32)
    r_m2 = nc.alloc_sbuf_tensor("r_m2", [1, 512], F32)
    r_var = nc.alloc_sbuf_tensor("r_var", [1, 512], F32)
    r_lnv = nc.alloc_sbuf_tensor("r_lnv", [1, 512], F32)
    r_rstd = nc.alloc_sbuf_tensor("r_rstd", [1, 512], F32)
    r_b = nc.alloc_sbuf_tensor("r_b", [1, 512], F32)
    bar_a = nc.alloc_sbuf_tensor("bar_a", [1, 64], F32)
    bar_v = nc.alloc_sbuf_tensor("bar_v", [1, 64], F32)
    bar_g = nc.alloc_sbuf_tensor("bar_g", [1, 64], F32)

    AR_A = nc.alloc_sbuf_tensor("AR_A", [P, 49152], BF16).ap()
    AR_B = nc.alloc_sbuf_tensor("AR_B", [P, 32768], BF16).ap()
    AR_C = nc.alloc_sbuf_tensor("AR_C", [P, 8192], BF16).ap()

    def _view(ar, off_kb, n_elem, dtype):
        o = off_kb * 512
        if dtype == BF16:
            return ar[:, o:o + n_elem]
        return ar[:, o:o + 2 * n_elem].bitcast(F32)

    def vA(off_kb, n, dt):
        return _view(AR_A, off_kb, n, dt)

    def vB(off_kb, n, dt):
        return _view(AR_B, off_kb, n, dt)

    def vC(off_kb, n, dt):
        return _view(AR_C, off_kb, n, dt)

    PF = [nc.alloc_psum_tensor(f"PF{i}", [P, 512], F32) for i in range(6)]
    PTF = [nc.alloc_psum_tensor(f"PTF{i}", [P, 512], F32) for i in range(2)]

    for s in ["s_pe", "s_ac", "s_ve", "s_po", "bar"]:
        b.new_sem(s)

    scratch_map = {"k_scr": (k_scr, [NH * P, S], BF16),
                   "v_scr": (v_scr, [NH * P, S], BF16),
                   "q_scr": (q_scr, [NH * P, SOWN], BF16),
                   "z_scr": (z_scr, [NH * P, SOWN], BF16),
                   "xm32_scr": (xm32_scr, [DC * P, SOWN], F32),
                   "up_scr": (up_scr, [16 * D, 512], BF16),
                   "down_scr": (down_scr, [4 * DH, 512], BF16)}

    def finish():
        for name in TAPS:
            src, shape, dt = scratch_map[name]
            tap = nc.dram_tensor("tap_" + name, shape, dt, kind="ExternalOutput").ap()
            nrow = shape[0]
            for r0 in range(0, nrow, 1024):
                b.dma("sync", tap[r0:r0 + 1024, :], src[r0:r0 + 1024, :], "tap")
        b.wait_all_dma("sync")
        with nc.Block() as blk:
            @blk.sync
            def _(e):
                for f in b.q["sync"]:
                    f(e)

            @blk.scalar
            def _(e):
                for f in b.q["scalar"]:
                    f(e)

            @blk.vector
            def _(e):
                for f in b.q["vector"]:
                    f(e)

            @blk.tensor
            def _(e):
                for f in b.q["tensor"]:
                    f(e)

            @blk.gpsimd
            def _(e):
                for f in b.q["gpsimd"]:
                    f(e)
        return nc

    bar_idx = [0]

    def barrier():
        o = 8 * bar_idx[0]
        bar_idx[0] += 1
        b.op("tensor", lambda e: e.matmul(
            PF[0][0:1, 0:1], onesb_sb[0:1, 0:1], onesb_sb[0:1, 0:1],
            start=True, stop=True, skip_group_check=True), "s_pe", pw=(0,))
        b.op("tensor", lambda e: e.matmul(
            PF[0][0:1, 0:1], onesb_sb[0:1, 0:1], onesb_sb[0:1, 0:1],
            start=True, stop=True, skip_group_check=True), "bar", pw=(0,))
        b.op("scalar", lambda e, o=o: e.memzero(bar_a[:, o:o + 4]), "s_ac")
        b.op("scalar", lambda e, o=o: e.memzero(bar_a[:, o + 4:o + 8]), "bar")
        b.op("vector", lambda e, o=o: e.memset(bar_v[:, o:o + 4], 0.0), "s_ve")
        b.op("vector", lambda e, o=o: e.memset(bar_v[:, o + 4:o + 8], 0.0), "bar")
        b.op("gpsimd", lambda e, o=o: e.memset(bar_g[:, o:o + 4], 0.0), "s_po")
        b.op("gpsimd", lambda e, o=o: e.memset(bar_g[:, o + 4:o + 8], 0.0), "bar")
        b.cnt["bar"] += 1
        sem = b.sem["bar"]
        b.emit("sync", lambda e, sem=sem: e.nop().then_inc(sem, 1))
        b.log.append(("op", "sync", "bar", b.cnt["bar"], (), ()))
        tgt = b.cnt["bar"]
        for eng in ENGS:
            b.wait(eng, "bar", tgt)

    AFCopy, AFExp, AFLn, AFRelu, AFSquare = AF.Copy, AF.Exp, AF.Ln, AF.Relu, AF.Square

    b.dma("sync", eyeb_sb[:, :], eye_bf, "ic")
    b.dma("sync", eyef_sb[:, :], eye_f32, "ic")
    b.dma("sync", onesb_sb[:, :], ones_bf, "ic")
    b.dma("sync", onesf_sb[:, :], ones_f32, "ic")
    b.dma("sync", nonesf_sb[:, :], nones_f32, "ic")
    b.dma("sync", mask_sb[:, :], mask512, "ic")
    b.dma("sync", scst_sb[0:64, :], upsc2d, "ic")
    c_init = b.dma("sync", osb_sb[0:1, 0:1], osc, "ic")

    b.wait("tensor", "ic", c_init)
    b.op("tensor", lambda e: e.matmul(PF[0][:, 0:64], scst_sb[0:64, :],
                                      eyef_sb[0:64, 0:64], start=True, stop=True), pw=(0,))
    m1 = b.op("tensor", lambda e: e.matmul(
        PF[1][:, 0:1], onesf_sb[0:1, :], osb_sb[0:1, 0:1], start=True, stop=True), "s_pe", pw=(1,))
    b.wait("vector", "s_pe", m1)
    b.op("vector", lambda e: e.tensor_copy(upsc_sb[:, :], PF[0][:, 0:64]), pr=(0,))
    v1 = b.op("vector", lambda e: e.tensor_copy(osb_sb[:, 0:1], PF[1][:, 0:1]), "s_ve", pr=(1,))
    b.wait("sync", "s_pe", m1)
    c_ds = b.dma("sync", scst_sb[0:16, :], dssc2d, "ic")
    b.wait("tensor", "ic", c_ds)
    b.wait("tensor", "s_ve", v1)
    m2 = b.op("tensor", lambda e: e.matmul(
        PF[0][:, 0:16], scst_sb[0:16, :], eyef_sb[0:16, 0:16],
        start=True, stop=True), "s_pe", pw=(0,))
    b.wait("vector", "s_pe", m2)
    v_dseff = b.op("vector", lambda e: e.tensor_scalar(
        dseff_sb[:, :], PF[0][:, 0:16], osb_sb[:, 0:1], None, ALU.mult), "s_ve", pr=(0,))

    wq_sb = vA(0, 8 * D, BF16)
    wk_sb = vA(32, 8 * D, BF16)
    wv_sb = vA(64, 8 * D, BF16)
    xk_sb = [vB(0, DC * SCW, BF16), vB(16, DC * SCW, BF16)]
    ek_sb = [vB(32, 512, BF16), vB(33, 512, BF16)]
    ev_sb = [vB(34, 512, BF16), vB(35, 512, BF16)]
    eq_sb = [vB(36, 512, BF16), vB(37, 512, BF16)]
    sgf_sb = [vB(40, D, F32), vB(48, D, F32)]
    sgb_sb = [vB(56, D, BF16), vB(60, D, BF16)]

    sign_units = [("up", rt, cc) for rt in range(16) for cc in range(4)] + \
                 [("dn", rt, 0) for rt in range(64)]
    sign_i = 0
    sign_loaded = {}
    sign_signed = {}
    sign_dmas = {}

    def emit_sign_unit(idx):
        kind, rt, cc = sign_units[idx]
        buf = idx % 2
        if (idx - 2) in sign_signed:
            b.wait("sync", "s_ac", sign_signed[idx - 2])
        if kind == "up":
            src = up_t[rt * P:(rt + 1) * P, cc * D:(cc + 1) * D]
        else:
            src = down_t[rt * P:(rt + 1) * P, :]
        sign_loaded[idx] = b.dma("sync", sgf_sb[buf][:, :], src, f"sg{buf}")
        b.wait("scalar", f"sg{buf}", sign_loaded[idx])
        if (idx - 2) in sign_dmas:
            b.wait("scalar", f"sw{buf}", sign_dmas[idx - 2])
        sign_signed[idx] = b.op(
            "scalar", lambda e, buf=buf: e.sign(sgb_sb[buf][:, :], sgf_sb[buf][:, :]), "s_ac")
        b.wait("scalar", "s_ac", sign_signed[idx])
        for qq in range(4):
            if kind == "up":
                blk = 4 * cc + qq
                dst = up_scr[blk * D + rt * P: blk * D + (rt + 1) * P, :]
            else:
                dst = down_scr[qq * DH + rt * P: qq * DH + (rt + 1) * P, :]
            sign_dmas[idx] = b.dma(
                "scalar", dst, sgb_sb[buf][:, qq * 512:(qq + 1) * 512], f"sw{buf}")

    ek_last = [0, 0]
    ev_last = [0, 0]
    eq_last = [0, 0]
    pe_after_xuse = {}

    for hg in range(2):
        if hg > 0:
            b.wait("gpsimd", "s_pe", b.cnt["s_pe"])
        cw = None
        for wsb, wsrc in ((wq_sb, wq_t), (wk_sb, wk_t), (wv_sb, wv_t)):
            for hh in range(8):
                h = hg * 8 + hh
                src = wsrc[h * D:(h + 1) * D, :].rearrange("(dc p) e -> p dc e", p=P)
                dst = wsb[:, hh * D:(hh + 1) * D].rearrange("p (dc e) -> p dc e", e=E)
                cw = b.dma("gpsimd", dst, src, "wld")
        b.wait("tensor", "wld", cw)
        if hg == 0:
            b.wait("tensor", "s_ve", v_dseff)
        for kind, n_chunks, xsrc in (("kv", 8, x_kv), ("q", 4, x_q)):
            for ci in range(n_chunks):
                xb = ci % 2
                prevkey = (hg, kind, ci - 2) if ci >= 2 else (hg, "kv", 6 + ci)
                if prevkey in pe_after_xuse:
                    b.wait("gpsimd", "s_pe", pe_after_xuse[prevkey])
                elif hg == 1:
                    b.wait("gpsimd", "s_pe", b.cnt["s_pe"])
                src = xsrc[:, ci * SCW:(ci + 1) * SCW].rearrange("(j p) s -> p j s", p=P)
                dst = xk_sb[xb][:, :].rearrange("p (j s) -> p j s", s=SCW)
                cx = b.dma("gpsimd", dst, src, f"x{xb}")
                b.wait("tensor", f"x{xb}", cx)
                for hh in range(8):
                    h = hg * 8 + hh
                    if kind == "kv":
                        specs = (("k", wk_sb, ek_sb, ek_last, k_scr, 0, "scalar", 1.0),
                                 ("v", wv_sb, ev_sb, ev_last, v_scr, 2, "vector", 1.0))
                    else:
                        specs = (("qq", wq_sb, eq_sb, eq_last, q_scr, 4, "scalar", QSCALE),)
                    for nm, wsb2, esb, elast, scr, bank0, evac_eng, scl in specs:
                        bank = PF[bank0 + (ci + hh) % 2]
                        mm = None
                        for dc in range(DC):
                            lhs = wsb2[:, hh * D + dc * E: hh * D + (dc + 1) * E]
                            rhs = xk_sb[xb][:, dc * SCW:(dc + 1) * SCW]
                            mm = b.op("tensor", lambda e, bank=bank, lhs=lhs, rhs=rhs,
                                      st=(dc == 0), sp=(dc == DC - 1): e.matmul(
                                bank[:, :], lhs, rhs, start=st, stop=sp),
                                "s_pe" if dc == DC - 1 else None,
                                pw=(bank0 + (ci + hh) % 2,))
                        eb = ci % 2
                        b.wait(evac_eng, "s_pe", mm)
                        b.wait(evac_eng, f"eo_{nm}{eb}", elast[eb])
                        if evac_eng == "scalar":
                            ev = b.op("scalar", lambda e, esb=esb, eb=eb, bank=bank, scl=scl:
                                      e.activation(esb[eb][:, :], bank[:, :], AFCopy, scale=scl),
                                      "s_ac", pr=(bank0 + (ci + hh) % 2,))
                            b.wait("sync", "s_ac", ev)
                            b.wait("tensor", "s_ac", ev)
                        else:
                            ev = b.op("vector", lambda e, esb=esb, eb=eb, bank=bank:
                                      e.tensor_copy(esb[eb][:, :], bank[:, :]), "s_ve",
                                      pr=(bank0 + (ci + hh) % 2,))
                            b.wait("sync", "s_ve", ev)
                            b.wait("tensor", "s_ve", ev)
                        elast[eb] = b.dma(
                            "sync", scr[h * P:(h + 1) * P, ci * SCW:(ci + 1) * SCW],
                            esb[eb][:, :], f"eo_{nm}{eb}")
                pe_after_xuse[(hg, kind, ci)] = b.cnt["s_pe"]
                if sign_i < len(sign_units):
                    emit_sign_unit(sign_i)
                    sign_i += 1
    while sign_i < len(sign_units):
        emit_sign_unit(sign_i)
        sign_i += 1

    b.wait_all_dma("sync")
    barrier()
    if NPHASES < 2:
        return finish()

    K_sb = [vA(0, S, BF16), vA(8, S, BF16)]
    V_sb = [vA(16, S, BF16), vA(24, S, BF16)]
    Q_sb = [vA(32, SOWN, BF16), vA(36, SOWN, BF16)]
    sc_sb = [vA(40, S, F32), vA(56, S, F32)]
    pt_sb = [vA(72, 512, BF16), vA(73, 512, BF16), vA(74, 512, BF16), vA(75, 512, BF16)]
    zq_sb = [vA(76, 512, BF16), vA(78, 512, BF16)]
    P_oct = vB(0, 18432, BF16)

    kq_loaded = [0, 0]
    v_loaded = [0, 0]
    head_done = {}

    def load_head(h):
        hb = h % 2
        if h >= 2:
            b.wait("sync", "s_pe", head_done[h - 2])
        b.dma("sync", K_sb[hb][:, :], k_scr[h * P:(h + 1) * P, :], f"kq{hb}")
        kq_loaded[hb] = b.dma("sync", Q_sb[hb][:, :], q_scr[h * P:(h + 1) * P, :], f"kq{hb}")
        if h >= 2:
            b.wait("scalar", "s_pe", head_done[h - 2])
        if PH1_MODE == "plainV":
            v_loaded[hb] = b.dma("scalar", V_sb[hb][:, :],
                                 v_scr[h * P:(h + 1) * P, :], f"v{hb}")
        else:
            for c in range(32):
                v_loaded[hb] = b.dma(
                    "scalar", V_sb[hb][:, c * E:(c + 1) * E],
                    v_scr[h * P:(h + 1) * P, c * E:(c + 1) * E], f"v{hb}", transpose=True)

    load_head(0)
    load_head(1)

    bank_last_evac = {0: ("s_ve", 0), 1: ("s_ve", 0)}
    t_bank_last_evac = {0: ("s_ve", 0), 1: ("s_ve", 0)}
    av_ms = {}
    z_bank_prev = {i: ("s_ve", 0) for i in range(4)}
    zq_last = [0, 0]
    prev_B_pe = 0

    for h in range(NH):
        hb = h % 2
        b.wait("tensor", f"kq{hb}", kq_loaded[hb])
        for oi, oct_slots in enumerate(OCT_SLOTS):
            offs = _slot_off(oct_slots)
            norm_done = {}
            first_slot = True
            for i in sorted(oct_slots):
                R = 256 * i
                ngrp = (R + 511) // 512
                icol = i - 1
                po = offs[i]
                grp_ev = []
                for g in range(ngrp):
                    w = min(512, R - g * 512)
                    bank = g % 2
                    engn, mi = bank_last_evac[bank]
                    b.wait("tensor", engn, mi)
                    mm = b.op("tensor", lambda e, bank=bank, hb=hb, i=i, g=g, w=w: e.matmul(
                        PF[bank][:, 0:w], Q_sb[hb][:, (i - 1) * P:i * P],
                        K_sb[hb][:, g * 512:g * 512 + w], start=True, stop=True,
                        skip_group_check=True), "s_pe", pw=(bank,))
                    last = (g == ngrp - 1)
                    if last:
                        b.wait("vector", "s_pe", mm)
                        mk = mask_sb[:, 0:512] if w == 512 else mask_sb[:, 256:512]
                        ev = b.op("vector", lambda e, g=g, w=w, bank=bank, mk=mk, hb=hb:
                                  e.tensor_add(sc_sb[hb][:, g * 512:g * 512 + w],
                                               PF[bank][:, 0:w], mk), "s_ve", pr=(bank,))
                        bank_last_evac[bank] = ("s_ve", ev)
                        grp_ev.append(("s_ve", ev))
                    elif g % 2 == 0:
                        b.wait("scalar", "s_pe", mm)
                        ev = b.op("scalar", lambda e, g=g, w=w, bank=bank, hb=hb:
                                  e.activation(sc_sb[hb][:, g * 512:g * 512 + w],
                                               PF[bank][:, 0:w], AFCopy), "s_ac", pr=(bank,))
                        bank_last_evac[bank] = ("s_ac", ev)
                        grp_ev.append(("s_ac", ev))
                    else:
                        b.wait("vector", "s_pe", mm)
                        ev = b.op("vector", lambda e, g=g, w=w, bank=bank, hb=hb:
                                  e.tensor_copy(sc_sb[hb][:, g * 512:g * 512 + w],
                                                PF[bank][:, 0:w]), "s_ve", pr=(bank,))
                        bank_last_evac[bank] = ("s_ve", ev)
                        grp_ev.append(("s_ve", ev))
                for engn, ev in grp_ev:
                    if engn == "s_ac":
                        b.wait("vector", "s_ac", ev)
                b.wait("vector", "s_ve", b.cnt["s_ve"])
                mx = b.op("vector", lambda e, R=R, icol=icol, hb=hb: e.reduce_max(
                    negmax_sb[:, icol:icol + 1], sc_sb[hb][:, 0:R],
                    axis=AX.X, negate=True), "s_ve")
                b.wait("scalar", "s_ve", mx)
                if first_slot:
                    b.wait("scalar", "s_pe", prev_B_pe)
                    first_slot = False
                b.wait("scalar", "s_ac", b.cnt["s_ac"])
                exp_m = b.op("scalar", lambda e, R=R, icol=icol, po=po, hb=hb: e.activation(
                    P_oct[:, po:po + R], sc_sb[hb][:, 0:R], AFExp,
                    bias=negmax_sb[:, icol:icol + 1],
                    accum_out=sums_sb[:, icol:icol + 1]))
                b.wait("scalar", "s_ac", exp_m)
                ln_m = b.op("scalar", lambda e, icol=icol: e.activation(
                    lnsum_sb[:, icol:icol + 1], sums_sb[:, icol:icol + 1], AFLn))
                b.wait("scalar", "s_ac", ln_m)
                rc = b.op("scalar", lambda e, icol=icol: e.activation(
                    recip_sb[:, icol:icol + 1], lnsum_sb[:, icol:icol + 1], AFExp,
                    scale=-1.0), "s_ac")
                b.wait("vector", "s_ac", rc)
                nm = b.op("vector", lambda e, R=R, po=po, icol=icol: e.tensor_scalar(
                    P_oct[:, po:po + R], P_oct[:, po:po + R],
                    recip_sb[:, icol:icol + 1], None, ALU.mult), "s_ve")
                norm_done[i] = nm
            if PH1_MODE == "noB":
                prev_B_pe = b.cnt["s_pe"]
                continue
            b.wait("tensor", "s_ve", max(norm_done.values()))
            b.wait("tensor", f"v{hb}", v_loaded[hb])
            imax = oct_slots[0]
            quads = [oct_slots[0:4], oct_slots[4:8]]
            zbank = [PF[2 + 2 * oi], PF[3 + 2 * oi]]
            for qi in range(2):
                engn, mi = z_bank_prev[2 * oi + qi]
                b.wait("tensor", engn, mi)
            av_done = {}
            for c in range(2 * imax):
                for qi, quad in enumerate(quads):
                    nvq = len([sl for sl in quad if 2 * sl > c])
                    if nvq == 0:
                        continue
                    pb = 2 * qi + c % 2
                    engn, mi = t_bank_last_evac[qi]
                    b.wait("tensor", engn, mi)
                    tr = None
                    for pos in range(nvq):
                        sl = quad[pos]
                        poff = offs[sl] + c * P
                        if PH1_MODE != "noT":
                            tr = b.op("tensor", lambda e, qi=qi, pos=pos, poff=poff: e.matmul(
                                PTF[qi][:, pos * P:(pos + 1) * P],
                                P_oct[:, poff:poff + P], eyeb_sb[:, :],
                                start=True, stop=True, skip_group_check=True),
                                "s_pe" if pos == nvq - 1 else None, pw=(6 + qi,))
                    if tr is None:
                        tr = b.op("tensor", lambda e: e.nop(), "s_pe")
                    if (c + qi) % 2 == 0:
                        b.wait("scalar", "s_pe", tr)
                        b.wait("scalar", "s_pe", av_ms.get((2 * oi + qi, c % 2), 0))
                        ev = b.op("scalar", lambda e, pb=pb, qi=qi, nvq=nvq: e.activation(
                            pt_sb[pb][:, 0:nvq * P], PTF[qi][:, 0:nvq * P], AFCopy), "s_ac",
                            pr=(6 + qi,))
                        t_bank_last_evac[qi] = ("s_ac", ev)
                        b.wait("tensor", "s_ac", ev)
                    else:
                        b.wait("vector", "s_pe", tr)
                        b.wait("vector", "s_pe", av_ms.get((2 * oi + qi, c % 2), 0))
                        ev = b.op("vector", lambda e, pb=pb, qi=qi, nvq=nvq: e.tensor_copy(
                            pt_sb[pb][:, 0:nvq * P], PTF[qi][:, 0:nvq * P]), "s_ve",
                            pr=(6 + qi,))
                        t_bank_last_evac[qi] = ("s_ve", ev)
                        b.wait("tensor", "s_ve", ev)
                    mm = b.op("tensor", lambda e, zb=zbank[qi], pb=pb, nvq=nvq,
                              st=(c == 0), sp=(c == 2 * quad[0] - 1), c=c, hb=hb: e.matmul(
                        zb[:, 0:nvq * P], V_sb[hb][:, c * E:(c + 1) * E],
                        pt_sb[pb][:, 0:nvq * P], start=st, stop=sp,
                        skip_group_check=True), "s_pe", pw=(2 + 2 * oi + qi,))
                    av_ms[(2 * oi + qi, c % 2)] = mm
                    if c == 2 * quad[0] - 1:
                        av_done[qi] = mm
            for qi, quad in enumerate(quads):
                b.wait("vector", "s_pe", av_done[qi])
                b.wait("vector", f"z{qi}", zq_last[qi])
                zev = b.op("vector", lambda e, zb=zbank[qi], qi=qi: e.tensor_copy(
                    zq_sb[qi][:, :], zb[:, :]), "s_ve", pr=(2 + 2 * oi + qi,))
                z_bank_prev[2 * oi + qi] = ("s_ve", zev)
                b.wait("gpsimd", "s_ve", zev)
                for pos, sl in enumerate(quad):
                    zq_last[qi] = b.dma(
                        "gpsimd", z_scr[h * P:(h + 1) * P, (sl - 1) * P:sl * P],
                        zq_sb[qi][:, pos * P:(pos + 1) * P], f"z{qi}")
            prev_B_pe = b.cnt["s_pe"]
        head_done[h] = b.cnt["s_pe"]
        if h + 2 < NH:
            load_head(h + 2)

    b.wait_all_dma("sync")
    barrier()
    if NPHASES < 3:
        return finish()

    wo_sb = vA(0, NH * D, BF16)
    zt_sb = vA(64, NH * 512, BF16)
    xq_sb = vB(0, DC * 512, F32)
    ares_sb = vB(32, DC * 512, F32)
    ab16_sb = vC(0, 512, BF16)
    asq_sb = vC(1, 512, BF16)
    xm32_sb = [vC(2, 512, F32), vC(4, 512, F32)]
    xm16_sb = [vC(6, 512, BF16), vC(7, 512, BF16)]
    xmt_sb = [vC(8, 512, F32), vC(10, 512, F32)]
    bca_sb = vC(12, 512, F32)
    bcb_sb = vC(14, 512, F32)

    cwo = None
    for hh in range(NH):
        cwo = b.dma("gpsimd", wo_sb[:, hh * D:(hh + 1) * D],
                    wo_t[hh * E:(hh + 1) * E, :], "wld")
    b.wait("tensor", "wld", cwo)

    def ln_rows(stat_pe_milestone, neg=True):
        b.wait("scalar", "s_pe", stat_pe_milestone)
        b.wait("scalar", "s_ve", b.cnt["s_ve"])
        b.wait("scalar", "s_ac", b.cnt["s_ac"])
        if PH2_RCUT < 1:
            return b.op("scalar", lambda e: e.memzero(bca_sb[:, 0:4]), "s_ac")
        mean_m = b.op("scalar", lambda e: e.activation(
            r_mean[:, :], PF[2][0:1, :], AFCopy, scale=1.0 / D), pr=(2,))
        b.op("scalar", lambda e: e.activation(r_sq[:, :], PF[3][0:1, :], AFCopy, scale=1.0 / D), pr=(3,))
        b.wait("scalar", "s_ac", mean_m)
        if PH2_RCUT < 2:
            return b.op("scalar", lambda e: e.memzero(bca_sb[:, 0:4]), "s_ac")
        m2m = b.op("scalar", lambda e: e.activation(r_m2[:, :], r_mean[:, :], AFSquare), "s_ac")
        if PH2_RCUT < 3:
            return b.op("scalar", lambda e: e.memzero(bca_sb[:, 0:4]), "s_ac")
        b.wait("vector", "s_ac", m2m)
        b.op("vector", lambda e: e.tensor_sub(r_var[:, :], r_sq[:, :], r_m2[:, :]))
        b.wait("vector", "s_ve", b.cnt["s_ve"])
        vv = b.op("vector", lambda e: e.tensor_scalar(
            r_lnv[:, :], r_var[:, :], 1e-5, None, ALU.add), "s_ve")
        if PH2_RCUT < 4:
            b.wait("scalar", "s_ve", vv)
            return b.op("scalar", lambda e: e.memzero(bca_sb[:, 0:4]), "s_ac")
        b.wait("scalar", "s_ve", vv)
        lnv_m = b.op("scalar", lambda e: e.activation(r_var[:, :], r_lnv[:, :], AFLn))
        b.wait("scalar", "s_ac", lnv_m)
        rst = b.op("scalar", lambda e: e.activation(
            r_rstd[:, :], r_var[:, :], AFExp, scale=-0.5), "s_ac")
        if PH2_RCUT < 5:
            return rst
        b.wait("vector", "s_ac", rst)
        rbm = b.op("vector", lambda e: e.tensor_mul(r_b[:, :], r_mean[:, :], r_rstd[:, :]), "s_ve")
        if PH2_RCUT < 6:
            b.wait("scalar", "s_ve", rbm)
            return b.op("scalar", lambda e: e.memzero(bca_sb[:, 0:4]), "s_ac")
        b.wait("tensor", "s_ac", rst)
        b.wait("tensor", "s_ve", rbm)
        b.op("tensor", lambda e: e.matmul(
            PF[4][:, :], onesf_sb[0:1, :], r_rstd[:, :], start=True, stop=True,
            skip_group_check=True), None, pw=(4,))
        bcb = b.op("tensor", lambda e: e.matmul(
            PF[5][:, :], nonesf_sb[0:1, :], r_b[:, :], start=True, stop=True,
            skip_group_check=True), "s_pe", pw=(5,))
        if PH2_RCUT < 7:
            b.wait("scalar", "s_pe", bcb)
            return b.op("scalar", lambda e: e.memzero(bca_sb[:, 0:4]), "s_ac")
        b.wait("scalar", "s_pe", bcb)
        b.op("scalar", lambda e: e.activation(bca_sb[:, :], PF[4][:, :], AFCopy), pr=(4,))
        ev5 = b.op("scalar", lambda e: e.activation(bcb_sb[:, :], PF[5][:, :], AFCopy), "s_ac", pr=(5,))
        return ev5

    xm_dma32 = {}
    xm_dma16 = {}
    prev_sc_pe = 0
    prev_sc_ve = 0
    for sc in range(PH2_NSC):
        b.wait("sync", "s_pe", prev_sc_pe)
        b.wait("sync", "s_ve", prev_sc_ve)
        czt = None
        for hh in range(NH):
            czt = b.dma("sync", zt_sb[:, hh * 512:(hh + 1) * 512],
                        z_scr[hh * P:(hh + 1) * P, sc * SCW:(sc + 1) * SCW], "p2l")
        cxq = b.dma("sync", xq_sb[:, :].rearrange("p (j s) -> p j s", s=SCW),
                    x_q[:, sc * SCW:(sc + 1) * SCW].rearrange("(j p) s -> p j s", p=P), "p2l")
        b.wait("tensor", "p2l", cxq)
        b.wait("tensor", "s_ve", prev_sc_ve)
        stat_mm = None
        for dc in range(DC):
            bank = dc % 2
            mm = None
            for hh in range(NH):
                mm = b.op("tensor", lambda e, bank=bank, hh=hh, dc=dc: e.matmul(
                    PF[bank][:, :], wo_sb[:, hh * D + dc * E: hh * D + (dc + 1) * E],
                    zt_sb[:, hh * 512:(hh + 1) * 512],
                    start=(hh == 0), stop=(hh == NH - 1), skip_group_check=True),
                    "s_pe" if hh == NH - 1 else None, pw=(bank,))
            b.wait("vector", "s_pe", mm)
            ar = b.op("vector", lambda e, dc=dc, bank=bank: e.tensor_add(
                ares_sb[:, dc * 512:(dc + 1) * 512], PF[bank][:, :],
                xq_sb[:, dc * 512:(dc + 1) * 512]), "s_ve", pr=(bank,))
            b.wait("tensor", "s_ve", ar)
            if PH2_PARTS == "ores":
                continue
            b.wait("vector", "s_ve", ar)
            cb = b.op("vector", lambda e, dc=dc: e.tensor_copy(
                ab16_sb[:, :], ares_sb[:, dc * 512:(dc + 1) * 512]), "s_ve")
            b.wait("scalar", "s_ve", cb)
            sq = b.op("scalar", lambda e, dc=dc: e.activation(
                asq_sb[:, :], ares_sb[:, dc * 512:(dc + 1) * 512], AFSquare), "s_ac")
            b.wait("tensor", "s_ve", cb)
            b.wait("tensor", "s_ac", sq)
            b.op("tensor", lambda e, dc=dc: e.matmul(
                PF[2][0:1, :], onesb_sb[:, 0:1], ab16_sb[:, :],
                start=(dc == 0), stop=(dc == DC - 1), skip_group_check=True), None, pw=(2,))
            stat_mm = b.op("tensor", lambda e, dc=dc: e.matmul(
                PF[3][0:1, :], onesb_sb[:, 0:1], asq_sb[:, :],
                start=(dc == 0), stop=(dc == DC - 1), skip_group_check=True),
                "s_pe", pw=(3,))
            if dc < DC - 1:
                b.wait("vector", "s_pe", stat_mm)
                b.wait("scalar", "s_pe", stat_mm)
        if PH2_PARTS in ("ores", "stats"):
            prev_sc_pe = b.cnt["s_pe"]
            prev_sc_ve = b.cnt["s_ve"]
            continue
        bev = ln_rows(stat_mm)
        if PH2_PARTS == "rows":
            prev_sc_pe = b.cnt["s_pe"]
            prev_sc_ve = b.cnt["s_ve"]
            continue
        b.wait("vector", "s_ac", bev)
        for dc in range(DC):
            gi = sc * DC + dc
            xb = dc % 2
            b.wait("vector", f"xm{xb}", xm_dma32.get(gi - 2, 0))
            t1 = b.op("vector", lambda e, dc=dc, xb=xb: e.tensor_mul(
                xmt_sb[xb][:, :], bca_sb[:, :], ares_sb[:, dc * 512:(dc + 1) * 512]), "s_ve")
            b.wait("vector", "s_ve", t1)
            t2 = b.op("vector", lambda e, xb=xb: e.tensor_add(
                xm32_sb[xb][:, :], bcb_sb[:, :], xmt_sb[xb][:, :]), "s_ve")
            if PH2_PARTS == "normnodma":
                continue
            b.wait("scalar", "s_ve", t2)
            b.wait("scalar", f"xm{xb}", xm_dma16.get(gi - 2, 0))
            c16 = b.op("scalar", lambda e, xb=xb: e.activation(
                xm16_sb[xb][:, :], xm32_sb[xb][:, :], AFCopy), "s_ac")
            if PH2_PARTS == "norm16":
                continue
            b.wait("sync", "s_ve", t2)
            xm_dma32[gi] = b.dma(
                "sync", xm32_scr[dc * P:(dc + 1) * P, sc * SCW:(sc + 1) * SCW],
                xm32_sb[xb][:, :], f"xm{xb}")
            b.wait("sync", "s_ac", c16)
            xm_dma16[gi] = b.dma(
                "sync", xmbf_scr[dc * P:(dc + 1) * P, sc * SCW:(sc + 1) * SCW],
                xm16_sb[xb][:, :], f"xm{xb}")
        prev_sc_pe = b.cnt["s_pe"]
        prev_sc_ve = b.cnt["s_ve"]

    b.wait_all_dma("sync")
    barrier()
    if NPHASES < 4:
        return finish()

    hid_sb = vA(0, 64 * 512, BF16)
    up_sb = [vA(64, DC * 512, BF16), vA(80, DC * 512, BF16)]
    xm16f_sb = vB(0, DC * 512, BF16)
    res_sb = vB(16, DC * 512, F32)
    dn_sb = [vB(48, 4 * 512, BF16), vB(52, 4 * 512, BF16)]
    xmr_sb = [vB(56, 512, F32), vB(60, 512, F32)]
    ffn_sb = [vC(2, 512, F32), vC(4, 512, F32)]
    out_sb = [vC(6, 512, F32), vC(8, 512, F32)]

    up_loaded = [0, 0]
    dn_pe_after = [0, 0]
    out_dma = {}
    rv_prev = [0, 0]
    prev_sc_pe3 = 0
    for sc in range(PH3_NSC):
        b.wait("sync", "s_pe", prev_sc_pe3)
        cxm = b.dma("sync", xm16f_sb[:, :].rearrange("p (j s) -> p j s", s=SCW),
                    xmbf_scr[:, sc * SCW:(sc + 1) * SCW].rearrange("(j p) s -> p j s", p=P),
                    "p3x")
        b.wait("tensor", "p3x", cxm)
        for hb_i in range(16):
            ub = hb_i % 2
            b.wait("sync", "s_pe", up_loaded[ub])
            cup = b.dma("sync", up_sb[ub][:, :].rearrange("p (dc h) -> p dc h", h=512),
                        up_scr[hb_i * D:(hb_i + 1) * D, :].rearrange("(dc p) h -> p dc h", p=P),
                        f"up{ub}")
            b.wait("tensor", f"up{ub}", cup)
            for ht in range(4):
                bank = ht % 2
                mm = None
                for dc in range(DC):
                    mm = b.op("tensor", lambda e, bank=bank, dc=dc, ht=ht, ub=ub: e.matmul(
                        PF[bank][:, :], up_sb[ub][:, dc * 512 + ht * P: dc * 512 + (ht + 1) * P],
                        xm16f_sb[:, dc * 512:(dc + 1) * 512],
                        start=(dc == 0), stop=(dc == DC - 1), skip_group_check=True),
                        "s_pe" if dc == DC - 1 else None, pw=(bank,))
                htg = hb_i * 4 + ht
                b.wait("scalar", "s_pe", mm)
                hv = b.op("scalar", lambda e, htg=htg, bank=bank: e.activation(
                    hid_sb[:, htg * 512:(htg + 1) * 512], PF[bank][:, :], AFRelu,
                    scale=upsc_sb[:, htg:htg + 1]), "s_ac", pr=(bank,))
                b.wait("tensor", "s_ac", hv)
            up_loaded[ub] = b.cnt["s_pe"]
        for dblk in (range(PH3_NDBLK) if PH3_PARTS in ("hiddown", "downmm", "downfv", "full") else []):
            mm_last = None
            for htq in range(PH3_NHTQ):
                db = htq % 2
                b.wait("sync", "s_pe", dn_pe_after[db])
                cdn = b.dma(
                    "sync", dn_sb[db][:, :].rearrange("p (t d) -> p t d", d=512),
                    down_scr[dblk * DH + htq * 512: dblk * DH + (htq + 1) * 512, :]
                    .rearrange("(t p) d -> p t d", p=P), f"dn{db}")
                b.wait("tensor", f"dn{db}", cdn)
                for tq in range(4):
                    ht64 = htq * 4 + tq
                    for dc4 in range(4):
                        mm_last = b.op("tensor", lambda e, dc4=dc4, tq=tq, db=db, ht64=ht64:
                                       e.matmul(
                            PF[2 + dc4][:, :],
                            dn_sb[db][:, tq * 512 + dc4 * P: tq * 512 + (dc4 + 1) * P],
                            hid_sb[:, ht64 * 512:(ht64 + 1) * 512],
                            start=(ht64 == 0), stop=(ht64 == PH3_NHTQ * 4 - 1), skip_group_check=True),
                            "s_pe" if (tq == 3 and dc4 == 3) else None,
                            pw=(2 + dc4,))
                dn_pe_after[db] = mm_last
            for dc4 in (range(4) if PH3_PARTS != "downmm" else []):
                dcg = dblk * 4 + dc4
                xb = dc4 % 2
                b.wait("scalar", "s_pe", mm_last)
                b.wait("scalar", "s_ve", rv_prev[xb])
                fv = b.op("scalar", lambda e, dc4=dc4, dcg=dcg, xb=xb: e.activation(
                    ffn_sb[xb][:, :], PF[2 + dc4][:, :], AFCopy,
                    scale=dseff_sb[:, dcg:dcg + 1]), "s_ac", pr=(2 + dc4,))
                fv_sem = "s_ac"
                if PH3_PARTS != "downfv":
                    b.wait("sync", "s_ve", rv_prev[xb])
                    cxr = b.dma("sync", xmr_sb[xb][:, :],
                                xm32_scr[dcg * P:(dcg + 1) * P, sc * SCW:(sc + 1) * SCW], f"xr{xb}")
                    b.wait("vector", fv_sem, fv)
                    b.wait("vector", f"xr{xb}", cxr)
                    rv = b.op("vector", lambda e, dcg=dcg, xb=xb: e.tensor_add(
                        res_sb[:, dcg * 512:(dcg + 1) * 512], ffn_sb[xb][:, :],
                        xmr_sb[xb][:, :]), "s_ve")
                    rv_prev[xb] = rv
                b.wait("tensor", fv_sem, fv)
        if PH3_PARTS != "full":
            continue
        stat_mm = None
        b.wait("vector", "s_ve", b.cnt["s_ve"])
        b.wait("scalar", "s_ve", b.cnt["s_ve"])
        for dc in range(DC):
            cb = b.op("vector", lambda e, dc=dc: e.tensor_copy(
                ab16_sb[:, :], res_sb[:, dc * 512:(dc + 1) * 512]), "s_ve")
            b.wait("scalar", "s_ve", cb)
            sq = b.op("scalar", lambda e, dc=dc: e.activation(
                asq_sb[:, :], res_sb[:, dc * 512:(dc + 1) * 512], AFSquare), "s_ac")
            b.wait("tensor", "s_ve", cb)
            b.wait("tensor", "s_ac", sq)
            b.op("tensor", lambda e, dc=dc: e.matmul(
                PF[2][0:1, :], onesb_sb[:, 0:1], ab16_sb[:, :],
                start=(dc == 0), stop=(dc == DC - 1), skip_group_check=True), None, pw=(2,))
            stat_mm = b.op("tensor", lambda e, dc=dc: e.matmul(
                PF[3][0:1, :], onesb_sb[:, 0:1], asq_sb[:, :],
                start=(dc == 0), stop=(dc == DC - 1), skip_group_check=True),
                "s_pe", pw=(3,))
            if dc < DC - 1:
                b.wait("vector", "s_pe", stat_mm)
                b.wait("scalar", "s_pe", stat_mm)
        bev = ln_rows(stat_mm)
        b.wait("vector", "s_ac", bev)
        for dc in range(DC):
            gi = sc * DC + dc
            xb = dc % 2
            b.wait("vector", f"ot{xb}", out_dma.get(gi - 2, 0))
            t1 = b.op("vector", lambda e, dc=dc, xb=xb: e.tensor_mul(
                ffn_sb[xb][:, :], bca_sb[:, :], res_sb[:, dc * 512:(dc + 1) * 512]), "s_ve")
            b.wait("vector", "s_ve", t1)
            t2 = b.op("vector", lambda e, xb=xb: e.tensor_add(
                out_sb[xb][:, :], bcb_sb[:, :], ffn_sb[xb][:, :]), "s_ve")
            b.wait("sync", "s_ve", t2)
            out_dma[gi] = b.dma(
                "sync", outT[dc * P:(dc + 1) * P, sc * SCW:(sc + 1) * SCW],
                out_sb[xb][:, :], f"ot{xb}")
        b.wait("tensor", "s_ve", b.cnt["s_ve"])
        prev_sc_pe3 = b.cnt["s_pe"]

    return finish()


def kernel(L0_attn_x, L0_attn_W_Q, L0_attn_W_K, L0_attn_W_V, L0_attn_W_O,
           L0_ffn_up_weight, L0_ffn_down_weight, L0_ffn_up_scale,
           L0_ffn_down_scale, L0_ffn_output_scale):
    x = np.asarray(L0_attn_x, dtype=np.float32)
    wq = np.asarray(L0_attn_W_Q, dtype=np.float32)
    wk = np.asarray(L0_attn_W_K, dtype=np.float32)
    wv = np.asarray(L0_attn_W_V, dtype=np.float32)
    wo = np.asarray(L0_attn_W_O, dtype=np.float32)
    upw = np.asarray(L0_ffn_up_weight, dtype=np.float32)
    dnw = np.asarray(L0_ffn_down_weight, dtype=np.float32)
    ups = np.asarray(L0_ffn_up_scale, dtype=np.float32)
    dns = np.asarray(L0_ffn_down_scale, dtype=np.float32)
    osc = np.asarray(L0_ffn_output_scale, dtype=np.float32)

    xT = np.ascontiguousarray(x.transpose(0, 2, 1))
    wq_t = np.ascontiguousarray(wq.transpose(0, 2, 1)).reshape(NH * D, E)
    wk_t = np.ascontiguousarray(wk.transpose(0, 2, 1)).reshape(NH * D, E)
    wv_t = np.ascontiguousarray(wv.transpose(0, 2, 1)).reshape(NH * D, E)
    wo_t = np.ascontiguousarray(wo.transpose(0, 2, 1)).reshape(NH * E, D)
    up_t = np.ascontiguousarray(upw.T)
    down_t = np.ascontiguousarray(dnw.T)
    upsc2d = np.ascontiguousarray(ups.reshape(64, P))
    dssc2d = np.ascontiguousarray(dns.reshape(16, P))
    osc2d = np.ascontiguousarray(osc.reshape(1, 1))

    eye_f32 = np.eye(P, dtype=np.float32)
    eye_bf = eye_f32.astype(ml_dtypes.bfloat16)
    ones_f32 = np.ones((P, P), dtype=np.float32)
    ones_bf = ones_f32.astype(ml_dtypes.bfloat16)
    nones_f32 = -ones_f32

    tri = np.triu(np.full((P, P), -1e9, dtype=np.float32), k=1)
    masks = []
    for par in range(2):
        m = np.zeros((P, 512), dtype=np.float32)
        if par == 0:
            m[:, 256:384] = tri
            m[:, 384:512] = -1e9
        else:
            m[:, 384:512] = tri
        masks.append(m)

    nc = build_program()

    in_maps = []
    for c in range(8):
        bb, par = c // 2, c % 2
        xb_T = xT[bb]
        x_q = np.ascontiguousarray(
            xb_T.reshape(D, 32, P)[:, par::2, :].reshape(D, SOWN))
        in_maps.append({
            "x_kv": np.ascontiguousarray(xb_T),
            "x_q": x_q,
            "wq_t": wq_t, "wk_t": wk_t, "wv_t": wv_t, "wo_t": wo_t,
            "up_t": up_t, "down_t": down_t,
            "upsc2d": upsc2d, "dssc2d": dssc2d, "osc": osc2d,
            "mask512": masks[par],
            "eye_bf": eye_bf, "eye_f32": eye_f32,
            "ones_bf": ones_bf, "ones_f32": ones_f32, "nones_f32": nones_f32,
        })

    res = run_bass_kernel_spmd(nc, in_maps, core_ids=list(range(8)))
    global LAST_RES
    LAST_RES = res

    out = np.empty((4, S, D), dtype=np.float32)
    for bb in range(4):
        full = np.empty((D, 32, P), dtype=np.float32)
        for par in range(2):
            shard = res.results[2 * bb + par]["outT"]
            full[:, par::2, :] = shard.reshape(D, NSLOT, P)
        out[bb] = full.reshape(D, S).T
    return out


# version 02
# speedup vs baseline: 1.0718x; 1.0718x over previous
import math
import numpy as np
import ml_dtypes
import concourse.bass as bass
import concourse.mybir as mybir
from concourse.bass_utils import run_bass_kernel_spmd

F32 = mybir.dt.float32
BF16 = mybir.dt.bfloat16
AF = mybir.ActivationFunctionType
ALU = mybir.AluOpType
AX = mybir.AxisListType

P = 128
D = 2048
DC = 16
NH = 16
E = 128
S = 4096
SOWN = 2048
DH = 8192
NSLOT = 16
SCW = 512
QSCALE = 1.0 / math.sqrt(128.0)

ENGS = ("sync", "scalar", "vector", "tensor", "gpsimd")
OCT_SLOTS = [[15, 13, 11, 9, 7, 5, 3, 1], [16, 14, 12, 10, 8, 6, 4, 2]]


def _slot_off(oct_slots):
    offs = {}
    o = 0
    for s in oct_slots:
        offs[s] = o
        o += 256 * s
    return offs


class Builder:
    def __init__(self):
        self.nc = bass.Bass("TRN2", target_bir_lowering=False)
        self.q = {e: [] for e in ENGS}
        self.cnt = {}
        self.sem = {}
        self.dma_sems = set()
        self.log = []

    def new_sem(self, name):
        self.sem[name] = self.nc.alloc_semaphore(name=name)
        self.cnt[name] = 0

    def emit(self, eng, f):
        self.q[eng].append(f)

    def dma(self, eng, dst, src, semname, **kw):
        if semname not in self.sem:
            self.new_sem(semname)
        self.dma_sems.add(semname)
        self.cnt[semname] += 16
        sem = self.sem[semname]

        def f(e, dst=dst, src=src, kw=kw, sem=sem):
            e.dma_start(dst, src, **kw).then_inc(sem, 16)
        self.emit(eng, f)
        self.log.append(("dma", eng, semname, self.cnt[semname], (), ()))
        return self.cnt[semname]

    def wait(self, eng, semname, val):
        if val is None or val <= 0:
            return
        assert val <= self.cnt[semname], (semname, val, self.cnt[semname])
        sem = self.sem[semname]

        def f(e, sem=sem, val=val):
            e.wait_ge(sem, val)
        self.emit(eng, f)
        self.log.append(("wait", eng, semname, val, (), ()))

    def wait_all_dma(self, eng):
        for s in sorted(self.dma_sems):
            self.wait(eng, s, self.cnt[s])

    ENGSEM = {"tensor": "s_pe", "scalar": "s_ac", "vector": "s_ve", "gpsimd": "s_po"}

    def op(self, eng, f, semname=None, pr=(), pw=()):
        self.log.append(("op", eng, semname,
                         self.cnt[semname] + 1 if semname else None, pr, pw))
        if semname is None:
            self.emit(eng, f)
            return None
        self.cnt[semname] += 1
        sem = self.sem[semname]

        def g(e, f=f, sem=sem):
            f(e).then_inc(sem, 1)
        self.emit(eng, g)
        return self.cnt[semname]


NPHASES = 4
PH1_MODE = "full"
PH3_PARTS = "full"
PH2_PARTS = "full"
PH2_RCUT = 9
PH2_NSC = 4
PH3_NDBLK = 4
PH3_NHTQ = 16
PH3_NSC = 4
PH3_FV = "act"
LAST_RES = None
TAPS = ()


def build_program():
    b = Builder()
    nc = b.nc

    x_kv = nc.dram_tensor("x_kv", [D, S], F32, kind="ExternalInput").ap()
    x_q = nc.dram_tensor("x_q", [D, SOWN], F32, kind="ExternalInput").ap()
    wq_t = nc.dram_tensor("wq_t", [NH * D, E], F32, kind="ExternalInput").ap()
    wk_t = nc.dram_tensor("wk_t", [NH * D, E], F32, kind="ExternalInput").ap()
    wv_t = nc.dram_tensor("wv_t", [NH * D, E], F32, kind="ExternalInput").ap()
    wo_t = nc.dram_tensor("wo_t", [NH * E, D], F32, kind="ExternalInput").ap()
    up_t = nc.dram_tensor("up_t", [D, DH], F32, kind="ExternalInput").ap()
    down_t = nc.dram_tensor("down_t", [DH, D], F32, kind="ExternalInput").ap()
    upsc2d = nc.dram_tensor("upsc2d", [64, P], F32, kind="ExternalInput").ap()
    dssc2d = nc.dram_tensor("dssc2d", [16, P], F32, kind="ExternalInput").ap()
    osc = nc.dram_tensor("osc", [1, 1], F32, kind="ExternalInput").ap()
    mask512 = nc.dram_tensor("mask512", [P, 512], F32, kind="ExternalInput").ap()
    eye_bf = nc.dram_tensor("eye_bf", [P, P], BF16, kind="ExternalInput").ap()
    eye_f32 = nc.dram_tensor("eye_f32", [P, P], F32, kind="ExternalInput").ap()
    ones_bf = nc.dram_tensor("ones_bf", [P, P], BF16, kind="ExternalInput").ap()
    ones_f32 = nc.dram_tensor("ones_f32", [P, P], F32, kind="ExternalInput").ap()
    nones_f32 = nc.dram_tensor("nones_f32", [P, P], F32, kind="ExternalInput").ap()
    outT = nc.dram_tensor("outT", [D, SOWN], F32, kind="ExternalOutput").ap()

    k_scr = nc.dram_tensor("k_scr", [NH * P, S], BF16, kind="Internal").ap()
    v_scr = nc.dram_tensor("v_scr", [NH * P, S], BF16, kind="Internal").ap()
    q_scr = nc.dram_tensor("q_scr", [NH * P, SOWN], BF16, kind="Internal").ap()
    z_scr = nc.dram_tensor("z_scr", [NH * P, SOWN], BF16, kind="Internal").ap()
    xm32_scr = nc.dram_tensor("xm32_scr", [DC * P, SOWN], F32, kind="Internal").ap()
    xmbf_scr = nc.dram_tensor("xmbf_scr", [DC * P, SOWN], BF16, kind="Internal").ap()
    up_scr = nc.dram_tensor("up_scr", [16 * D, 512], BF16, kind="Internal").ap()
    down_scr = nc.dram_tensor("down_scr", [4 * DH, 512], BF16, kind="Internal").ap()

    eyeb_sb = nc.alloc_sbuf_tensor("eyeb_sb", [P, P], BF16)
    eyef_sb = nc.alloc_sbuf_tensor("eyef_sb", [P, P], F32)
    onesb_sb = nc.alloc_sbuf_tensor("onesb_sb", [P, P], BF16)
    onesf_sb = nc.alloc_sbuf_tensor("onesf_sb", [P, P], F32)
    nonesf_sb = nc.alloc_sbuf_tensor("nonesf_sb", [P, P], F32)
    mask_sb = nc.alloc_sbuf_tensor("mask_sb", [P, 512], F32)
    upsc_sb = nc.alloc_sbuf_tensor("upsc_sb", [P, 64], F32)
    dseff_sb = nc.alloc_sbuf_tensor("dseff_sb", [P, 16], F32)
    scst_sb = nc.alloc_sbuf_tensor("scst_sb", [64, P], F32)
    osb_sb = nc.alloc_sbuf_tensor("osb_sb", [P, 1], F32)
    negmax_sb = nc.alloc_sbuf_tensor("negmax_sb", [P, 16], F32)
    sums_sb = nc.alloc_sbuf_tensor("sums_sb", [P, 16], F32)
    lnsum_sb = nc.alloc_sbuf_tensor("lnsum_sb", [P, 16], F32)
    recip_sb = nc.alloc_sbuf_tensor("recip_sb", [P, 16], F32)
    r_sum = nc.alloc_sbuf_tensor("r_sum", [1, 512], F32)
    r_sq = nc.alloc_sbuf_tensor("r_sq", [1, 512], F32)
    r_mean = nc.alloc_sbuf_tensor("r_mean", [1, 512], F32)
    r_m2 = nc.alloc_sbuf_tensor("r_m2", [1, 512], F32)
    r_var = nc.alloc_sbuf_tensor("r_var", [1, 512], F32)
    r_lnv = nc.alloc_sbuf_tensor("r_lnv", [1, 512], F32)
    r_rstd = nc.alloc_sbuf_tensor("r_rstd", [1, 512], F32)
    r_b = nc.alloc_sbuf_tensor("r_b", [1, 512], F32)
    bar_a = nc.alloc_sbuf_tensor("bar_a", [1, 64], F32)
    bar_v = nc.alloc_sbuf_tensor("bar_v", [1, 64], F32)
    bar_g = nc.alloc_sbuf_tensor("bar_g", [1, 64], F32)

    AR_A = nc.alloc_sbuf_tensor("AR_A", [P, 49152], BF16).ap()
    AR_B = nc.alloc_sbuf_tensor("AR_B", [P, 32768], BF16).ap()
    AR_C = nc.alloc_sbuf_tensor("AR_C", [P, 8192], BF16).ap()

    def _view(ar, off_kb, n_elem, dtype):
        o = off_kb * 512
        if dtype == BF16:
            return ar[:, o:o + n_elem]
        return ar[:, o:o + 2 * n_elem].bitcast(F32)

    def vA(off_kb, n, dt):
        return _view(AR_A, off_kb, n, dt)

    def vB(off_kb, n, dt):
        return _view(AR_B, off_kb, n, dt)

    def vC(off_kb, n, dt):
        return _view(AR_C, off_kb, n, dt)

    PF = [nc.alloc_psum_tensor(f"PF{i}", [P, 512], F32) for i in range(6)]
    PTF = [nc.alloc_psum_tensor(f"PTF{i}", [P, 512], F32) for i in range(2)]

    for s in ["s_pe", "s_ac", "s_ve", "s_po", "bar"]:
        b.new_sem(s)

    scratch_map = {"k_scr": (k_scr, [NH * P, S], BF16),
                   "v_scr": (v_scr, [NH * P, S], BF16),
                   "q_scr": (q_scr, [NH * P, SOWN], BF16),
                   "z_scr": (z_scr, [NH * P, SOWN], BF16),
                   "xm32_scr": (xm32_scr, [DC * P, SOWN], F32),
                   "up_scr": (up_scr, [16 * D, 512], BF16),
                   "down_scr": (down_scr, [4 * DH, 512], BF16)}

    def finish():
        for name in TAPS:
            src, shape, dt = scratch_map[name]
            tap = nc.dram_tensor("tap_" + name, shape, dt, kind="ExternalOutput").ap()
            nrow = shape[0]
            for r0 in range(0, nrow, 1024):
                b.dma("sync", tap[r0:r0 + 1024, :], src[r0:r0 + 1024, :], "tap")
        b.wait_all_dma("sync")
        with nc.Block() as blk:
            @blk.sync
            def _(e):
                for f in b.q["sync"]:
                    f(e)

            @blk.scalar
            def _(e):
                for f in b.q["scalar"]:
                    f(e)

            @blk.vector
            def _(e):
                for f in b.q["vector"]:
                    f(e)

            @blk.tensor
            def _(e):
                for f in b.q["tensor"]:
                    f(e)

            @blk.gpsimd
            def _(e):
                for f in b.q["gpsimd"]:
                    f(e)
        return nc

    bar_idx = [0]

    def barrier():
        o = 8 * bar_idx[0]
        bar_idx[0] += 1
        b.op("tensor", lambda e: e.matmul(
            PF[0][0:1, 0:1], onesb_sb[0:1, 0:1], onesb_sb[0:1, 0:1],
            start=True, stop=True, skip_group_check=True), "s_pe", pw=(0,))
        b.op("tensor", lambda e: e.matmul(
            PF[0][0:1, 0:1], onesb_sb[0:1, 0:1], onesb_sb[0:1, 0:1],
            start=True, stop=True, skip_group_check=True), "bar", pw=(0,))
        b.op("scalar", lambda e, o=o: e.memzero(bar_a[:, o:o + 4]), "s_ac")
        b.op("scalar", lambda e, o=o: e.memzero(bar_a[:, o + 4:o + 8]), "bar")
        b.op("vector", lambda e, o=o: e.memset(bar_v[:, o:o + 4], 0.0), "s_ve")
        b.op("vector", lambda e, o=o: e.memset(bar_v[:, o + 4:o + 8], 0.0), "bar")
        b.op("gpsimd", lambda e, o=o: e.memset(bar_g[:, o:o + 4], 0.0), "s_po")
        b.op("gpsimd", lambda e, o=o: e.memset(bar_g[:, o + 4:o + 8], 0.0), "bar")
        b.cnt["bar"] += 1
        sem = b.sem["bar"]
        b.emit("sync", lambda e, sem=sem: e.nop().then_inc(sem, 1))
        b.log.append(("op", "sync", "bar", b.cnt["bar"], (), ()))
        tgt = b.cnt["bar"]
        for eng in ENGS:
            b.wait(eng, "bar", tgt)

    AFCopy, AFExp, AFLn, AFRelu, AFSquare = AF.Copy, AF.Exp, AF.Ln, AF.Relu, AF.Square

    b.dma("sync", eyeb_sb[:, :], eye_bf, "ic")
    b.dma("sync", eyef_sb[:, :], eye_f32, "ic")
    b.dma("sync", onesb_sb[:, :], ones_bf, "ic")
    b.dma("sync", onesf_sb[:, :], ones_f32, "ic")
    b.dma("sync", nonesf_sb[:, :], nones_f32, "ic")
    b.dma("sync", mask_sb[:, :], mask512, "ic")
    b.dma("sync", scst_sb[0:64, :], upsc2d, "ic")
    c_init = b.dma("sync", osb_sb[0:1, 0:1], osc, "ic")

    b.wait("tensor", "ic", c_init)
    b.op("tensor", lambda e: e.matmul(PF[0][:, 0:64], scst_sb[0:64, :],
                                      eyef_sb[0:64, 0:64], start=True, stop=True), pw=(0,))
    m1 = b.op("tensor", lambda e: e.matmul(
        PF[1][:, 0:1], onesf_sb[0:1, :], osb_sb[0:1, 0:1], start=True, stop=True), "s_pe", pw=(1,))
    b.wait("vector", "s_pe", m1)
    b.op("vector", lambda e: e.tensor_copy(upsc_sb[:, :], PF[0][:, 0:64]), pr=(0,))
    v1 = b.op("vector", lambda e: e.tensor_copy(osb_sb[:, 0:1], PF[1][:, 0:1]), "s_ve", pr=(1,))
    b.wait("sync", "s_pe", m1)
    c_ds = b.dma("sync", scst_sb[0:16, :], dssc2d, "ic")
    b.wait("tensor", "ic", c_ds)
    b.wait("tensor", "s_ve", v1)
    m2 = b.op("tensor", lambda e: e.matmul(
        PF[0][:, 0:16], scst_sb[0:16, :], eyef_sb[0:16, 0:16],
        start=True, stop=True), "s_pe", pw=(0,))
    b.wait("vector", "s_pe", m2)
    v_dseff = b.op("vector", lambda e: e.tensor_scalar(
        dseff_sb[:, :], PF[0][:, 0:16], osb_sb[:, 0:1], None, ALU.mult), "s_ve", pr=(0,))

    wq_sb = vA(0, 8 * D, BF16)
    wk_sb = vA(32, 8 * D, BF16)
    wv_sb = vA(64, 8 * D, BF16)
    xk_sb = [vB(0, DC * SCW, BF16), vB(16, DC * SCW, BF16)]
    ek_sb = [vB(32, 512, BF16), vB(33, 512, BF16)]
    ev_sb = [vB(34, 512, BF16), vB(35, 512, BF16)]
    eq_sb = [vB(36, 512, BF16), vB(37, 512, BF16)]
    sgf_sb = [vB(40, D, F32), vB(48, D, F32)]
    sgb_sb = [vB(56, D, BF16), vB(60, D, BF16)]

    sign_units = [("up", rt, cc) for rt in range(16) for cc in range(4)] + \
                 [("dn", rt, 0) for rt in range(64)]
    sign_i = 0
    sign_loaded = {}
    sign_signed = {}
    sign_dmas = {}

    def emit_sign_unit(idx):
        kind, rt, cc = sign_units[idx]
        buf = idx % 2
        if (idx - 2) in sign_signed:
            b.wait("sync", "s_ac", sign_signed[idx - 2])
        if kind == "up":
            src = up_t[rt * P:(rt + 1) * P, cc * D:(cc + 1) * D]
        else:
            src = down_t[rt * P:(rt + 1) * P, :]
        sign_loaded[idx] = b.dma("sync", sgf_sb[buf][:, :], src, f"sg{buf}")
        b.wait("scalar", f"sg{buf}", sign_loaded[idx])
        if (idx - 2) in sign_dmas:
            b.wait("scalar", f"sw{buf}", sign_dmas[idx - 2])
        sign_signed[idx] = b.op(
            "scalar", lambda e, buf=buf: e.sign(sgb_sb[buf][:, :], sgf_sb[buf][:, :]), "s_ac")
        b.wait("scalar", "s_ac", sign_signed[idx])
        for qq in range(4):
            if kind == "up":
                blk = 4 * cc + qq
                dst = up_scr[blk * D + rt * P: blk * D + (rt + 1) * P, :]
            else:
                dst = down_scr[qq * DH + rt * P: qq * DH + (rt + 1) * P, :]
            sign_dmas[idx] = b.dma(
                "scalar", dst, sgb_sb[buf][:, qq * 512:(qq + 1) * 512], f"sw{buf}")

    ek_last = [0, 0]
    ev_last = [0, 0]
    eq_last = [0, 0]
    pe_after_xuse = {}
    ph0_guard = {0: ("s_ve", 0), 1: ("s_ve", 0)}

    for hg in range(2):
        if hg > 0:
            b.wait("gpsimd", "s_pe", b.cnt["s_pe"])
        cw = None
        for wsb, wsrc in ((wq_sb, wq_t), (wk_sb, wk_t), (wv_sb, wv_t)):
            for hh in range(8):
                h = hg * 8 + hh
                src = wsrc[h * D:(h + 1) * D, :].rearrange("(dc p) e -> p dc e", p=P)
                dst = wsb[:, hh * D:(hh + 1) * D].rearrange("p (dc e) -> p dc e", e=E)
                cw = b.dma("gpsimd", dst, src, "wld")
        b.wait("tensor", "wld", cw)
        if hg == 0:
            ph0_guard[0] = ("s_ve", v_dseff)
            ph0_guard[1] = ("s_ve", v_dseff)
        for kind, n_chunks, xsrc in (("kv", 8, x_kv), ("q", 4, x_q)):
            for ci in range(n_chunks):
                xb = ci % 2
                prevkey = (hg, kind, ci - 2) if ci >= 2 else (hg, "kv", 6 + ci)
                if prevkey in pe_after_xuse:
                    b.wait("gpsimd", "s_pe", pe_after_xuse[prevkey])
                elif hg == 1:
                    b.wait("gpsimd", "s_pe", b.cnt["s_pe"])
                src = xsrc[:, ci * SCW:(ci + 1) * SCW].rearrange("(j p) s -> p j s", p=P)
                dst = xk_sb[xb][:, :].rearrange("p (j s) -> p j s", s=SCW)
                cx = b.dma("gpsimd", dst, src, f"x{xb}")
                b.wait("tensor", f"x{xb}", cx)
                for hh in range(8):
                    h = hg * 8 + hh
                    if kind == "kv":
                        specs = (("k", wk_sb, ek_sb, ek_last, k_scr, 0, "scalar", 1.0),
                                 ("v", wv_sb, ev_sb, ev_last, v_scr, 2, "vector", 1.0))
                    else:
                        specs = (("qq", wq_sb, eq_sb, eq_last, q_scr, 4, "scalar", QSCALE),)
                    for nm, wsb2, esb, elast, scr, bank0, evac_eng, scl in specs:
                        bidx0 = bank0 + (ci + hh) % 2
                        bank = PF[bidx0]
                        engn, mi = ph0_guard.get(bidx0, (None, 0))
                        if engn:
                            b.wait("tensor", engn, mi)
                        mm = None
                        for dc in range(DC):
                            lhs = wsb2[:, hh * D + dc * E: hh * D + (dc + 1) * E]
                            rhs = xk_sb[xb][:, dc * SCW:(dc + 1) * SCW]
                            mm = b.op("tensor", lambda e, bank=bank, lhs=lhs, rhs=rhs,
                                      st=(dc == 0), sp=(dc == DC - 1): e.matmul(
                                bank[:, :], lhs, rhs, start=st, stop=sp),
                                "s_pe" if dc == DC - 1 else None,
                                pw=(bank0 + (ci + hh) % 2,))
                        eb = ci % 2
                        b.wait(evac_eng, "s_pe", mm)
                        b.wait(evac_eng, f"eo_{nm}{eb}", elast[eb])
                        if evac_eng == "scalar":
                            ev = b.op("scalar", lambda e, esb=esb, eb=eb, bank=bank, scl=scl:
                                      e.activation(esb[eb][:, :], bank[:, :], AFCopy, scale=scl),
                                      "s_ac", pr=(bank0 + (ci + hh) % 2,))
                            b.wait("sync", "s_ac", ev)
                            ph0_guard[bidx0] = ("s_ac", ev)
                        else:
                            ev = b.op("vector", lambda e, esb=esb, eb=eb, bank=bank:
                                      e.tensor_copy(esb[eb][:, :], bank[:, :]), "s_ve",
                                      pr=(bank0 + (ci + hh) % 2,))
                            b.wait("sync", "s_ve", ev)
                            ph0_guard[bidx0] = ("s_ve", ev)
                        elast[eb] = b.dma(
                            "sync", scr[h * P:(h + 1) * P, ci * SCW:(ci + 1) * SCW],
                            esb[eb][:, :], f"eo_{nm}{eb}")
                pe_after_xuse[(hg, kind, ci)] = b.cnt["s_pe"]
                if sign_i < len(sign_units):
                    emit_sign_unit(sign_i)
                    sign_i += 1
    while sign_i < len(sign_units):
        emit_sign_unit(sign_i)
        sign_i += 1

    b.wait_all_dma("sync")
    barrier()
    if NPHASES < 2:
        return finish()

    K_sb = [vA(0, S, BF16), vA(8, S, BF16)]
    V_sb = [vA(16, S, BF16), vA(24, S, BF16)]
    Q_sb = [vA(32, SOWN, BF16), vA(36, SOWN, BF16)]
    sc_sb = [vA(40, S, F32), vA(56, S, F32)]
    pt_sb = [vA(72, 512, BF16), vA(73, 512, BF16), vA(74, 512, BF16), vA(75, 512, BF16)]
    zq_sb = [vA(76, 512, BF16), vA(78, 512, BF16)]
    P_oct = vB(0, 18432, BF16)

    kq_loaded = [0, 0]
    v_loaded = [0, 0]
    head_done = {}

    def load_head(h):
        hb = h % 2
        if h >= 2:
            b.wait("sync", "s_pe", head_done[h - 2])
        b.dma("sync", K_sb[hb][:, :], k_scr[h * P:(h + 1) * P, :], f"kq{hb}")
        kq_loaded[hb] = b.dma("sync", Q_sb[hb][:, :], q_scr[h * P:(h + 1) * P, :], f"kq{hb}")
        if h >= 2:
            b.wait("scalar", "s_pe", head_done[h - 2])
        if PH1_MODE == "plainV":
            v_loaded[hb] = b.dma("scalar", V_sb[hb][:, :],
                                 v_scr[h * P:(h + 1) * P, :], f"v{hb}")
        else:
            for c in range(32):
                v_loaded[hb] = b.dma(
                    "scalar", V_sb[hb][:, c * E:(c + 1) * E],
                    v_scr[h * P:(h + 1) * P, c * E:(c + 1) * E], f"v{hb}", transpose=True)

    load_head(0)
    load_head(1)

    bank_last_evac = {0: ("s_ve", 0), 1: ("s_ve", 0)}
    t_bank_last_evac = {0: ("s_ve", 0), 1: ("s_ve", 0)}
    av_ms = {}
    z_bank_prev = {i: ("s_ve", 0) for i in range(4)}
    zq_last = [0, 0]
    prev_B_pe = 0

    for h in range(NH):
        hb = h % 2
        b.wait("tensor", f"kq{hb}", kq_loaded[hb])
        for oi, oct_slots in enumerate(OCT_SLOTS):
            offs = _slot_off(oct_slots)
            norm_done = {}
            first_slot = True
            for i in sorted(oct_slots):
                R = 256 * i
                ngrp = (R + 511) // 512
                icol = i - 1
                po = offs[i]
                grp_ev = []
                for g in range(ngrp):
                    w = min(512, R - g * 512)
                    bank = g % 2
                    engn, mi = bank_last_evac[bank]
                    b.wait("tensor", engn, mi)
                    mm = b.op("tensor", lambda e, bank=bank, hb=hb, i=i, g=g, w=w: e.matmul(
                        PF[bank][:, 0:w], Q_sb[hb][:, (i - 1) * P:i * P],
                        K_sb[hb][:, g * 512:g * 512 + w], start=True, stop=True,
                        skip_group_check=True), "s_pe", pw=(bank,))
                    last = (g == ngrp - 1)
                    if last:
                        b.wait("vector", "s_pe", mm)
                        mk = mask_sb[:, 0:512] if w == 512 else mask_sb[:, 256:512]
                        ev = b.op("vector", lambda e, g=g, w=w, bank=bank, mk=mk, hb=hb:
                                  e.tensor_add(sc_sb[hb][:, g * 512:g * 512 + w],
                                               PF[bank][:, 0:w], mk), "s_ve", pr=(bank,))
                        bank_last_evac[bank] = ("s_ve", ev)
                        grp_ev.append(("s_ve", ev))
                    elif g % 2 == 0:
                        b.wait("scalar", "s_pe", mm)
                        ev = b.op("scalar", lambda e, g=g, w=w, bank=bank, hb=hb:
                                  e.activation(sc_sb[hb][:, g * 512:g * 512 + w],
                                               PF[bank][:, 0:w], AFCopy), "s_ac", pr=(bank,))
                        bank_last_evac[bank] = ("s_ac", ev)
                        grp_ev.append(("s_ac", ev))
                    else:
                        b.wait("vector", "s_pe", mm)
                        ev = b.op("vector", lambda e, g=g, w=w, bank=bank, hb=hb:
                                  e.tensor_copy(sc_sb[hb][:, g * 512:g * 512 + w],
                                                PF[bank][:, 0:w]), "s_ve", pr=(bank,))
                        bank_last_evac[bank] = ("s_ve", ev)
                        grp_ev.append(("s_ve", ev))
                for engn, ev in grp_ev:
                    if engn == "s_ac":
                        b.wait("vector", "s_ac", ev)
                b.wait("vector", "s_ve", b.cnt["s_ve"])
                mx = b.op("vector", lambda e, R=R, icol=icol, hb=hb: e.reduce_max(
                    negmax_sb[:, icol:icol + 1], sc_sb[hb][:, 0:R],
                    axis=AX.X, negate=True), "s_ve")
                b.wait("scalar", "s_ve", mx)
                if first_slot:
                    b.wait("scalar", "s_pe", prev_B_pe)
                    first_slot = False
                b.wait("scalar", "s_ac", b.cnt["s_ac"])
                exp_m = b.op("scalar", lambda e, R=R, icol=icol, po=po, hb=hb: e.activation(
                    P_oct[:, po:po + R], sc_sb[hb][:, 0:R], AFExp,
                    bias=negmax_sb[:, icol:icol + 1],
                    accum_out=sums_sb[:, icol:icol + 1]))
                b.wait("scalar", "s_ac", exp_m)
                ln_m = b.op("scalar", lambda e, icol=icol: e.activation(
                    lnsum_sb[:, icol:icol + 1], sums_sb[:, icol:icol + 1], AFLn))
                b.wait("scalar", "s_ac", ln_m)
                rc = b.op("scalar", lambda e, icol=icol: e.activation(
                    recip_sb[:, icol:icol + 1], lnsum_sb[:, icol:icol + 1], AFExp,
                    scale=-1.0), "s_ac")
                b.wait("vector", "s_ac", rc)
                nm = b.op("vector", lambda e, R=R, po=po, icol=icol: e.tensor_scalar(
                    P_oct[:, po:po + R], P_oct[:, po:po + R],
                    recip_sb[:, icol:icol + 1], None, ALU.mult), "s_ve")
                norm_done[i] = nm
            if PH1_MODE == "noB":
                prev_B_pe = b.cnt["s_pe"]
                continue
            b.wait("tensor", "s_ve", max(norm_done.values()))
            b.wait("tensor", f"v{hb}", v_loaded[hb])
            imax = oct_slots[0]
            quads = [oct_slots[0:4], oct_slots[4:8]]
            zbank = [PF[2 + 2 * oi], PF[3 + 2 * oi]]
            for qi in range(2):
                engn, mi = z_bank_prev[2 * oi + qi]
                b.wait("tensor", engn, mi)
            av_done = {}
            for c in range(2 * imax):
                for qi, quad in enumerate(quads):
                    nvq = len([sl for sl in quad if 2 * sl > c])
                    if nvq == 0:
                        continue
                    pb = 2 * qi + c % 2
                    engn, mi = t_bank_last_evac[qi]
                    b.wait("tensor", engn, mi)
                    tr = None
                    for pos in range(nvq):
                        sl = quad[pos]
                        poff = offs[sl] + c * P
                        if PH1_MODE != "noT":
                            tr = b.op("tensor", lambda e, qi=qi, pos=pos, poff=poff: e.matmul(
                                PTF[qi][:, pos * P:(pos + 1) * P],
                                P_oct[:, poff:poff + P], eyeb_sb[:, :],
                                start=True, stop=True, skip_group_check=True),
                                "s_pe" if pos == nvq - 1 else None, pw=(6 + qi,))
                    if tr is None:
                        tr = b.op("tensor", lambda e: e.nop(), "s_pe")
                    if (c + qi) % 2 == 0:
                        b.wait("scalar", "s_pe", tr)
                        b.wait("scalar", "s_pe", av_ms.get((2 * oi + qi, c % 2), 0))
                        ev = b.op("scalar", lambda e, pb=pb, qi=qi, nvq=nvq: e.activation(
                            pt_sb[pb][:, 0:nvq * P], PTF[qi][:, 0:nvq * P], AFCopy), "s_ac",
                            pr=(6 + qi,))
                        t_bank_last_evac[qi] = ("s_ac", ev)
                        b.wait("tensor", "s_ac", ev)
                    else:
                        b.wait("vector", "s_pe", tr)
                        b.wait("vector", "s_pe", av_ms.get((2 * oi + qi, c % 2), 0))
                        ev = b.op("vector", lambda e, pb=pb, qi=qi, nvq=nvq: e.tensor_copy(
                            pt_sb[pb][:, 0:nvq * P], PTF[qi][:, 0:nvq * P]), "s_ve",
                            pr=(6 + qi,))
                        t_bank_last_evac[qi] = ("s_ve", ev)
                        b.wait("tensor", "s_ve", ev)
                    mm = b.op("tensor", lambda e, zb=zbank[qi], pb=pb, nvq=nvq,
                              st=(c == 0), sp=(c == 2 * quad[0] - 1), c=c, hb=hb: e.matmul(
                        zb[:, 0:nvq * P], V_sb[hb][:, c * E:(c + 1) * E],
                        pt_sb[pb][:, 0:nvq * P], start=st, stop=sp,
                        skip_group_check=True), "s_pe", pw=(2 + 2 * oi + qi,))
                    av_ms[(2 * oi + qi, c % 2)] = mm
                    if c == 2 * quad[0] - 1:
                        av_done[qi] = mm
            for qi, quad in enumerate(quads):
                b.wait("vector", "s_pe", av_done[qi])
                b.wait("vector", f"z{qi}", zq_last[qi])
                zev = b.op("vector", lambda e, zb=zbank[qi], qi=qi: e.tensor_copy(
                    zq_sb[qi][:, :], zb[:, :]), "s_ve", pr=(2 + 2 * oi + qi,))
                z_bank_prev[2 * oi + qi] = ("s_ve", zev)
                b.wait("gpsimd", "s_ve", zev)
                for pos, sl in enumerate(quad):
                    zq_last[qi] = b.dma(
                        "gpsimd", z_scr[h * P:(h + 1) * P, (sl - 1) * P:sl * P],
                        zq_sb[qi][:, pos * P:(pos + 1) * P], f"z{qi}")
            prev_B_pe = b.cnt["s_pe"]
        head_done[h] = b.cnt["s_pe"]
        if h + 2 < NH:
            load_head(h + 2)

    b.wait_all_dma("sync")
    barrier()
    if NPHASES < 3:
        return finish()

    wo_sb = vA(0, NH * D, BF16)
    zt_sb = vA(64, NH * 512, BF16)
    xq_sb = vB(0, DC * 512, F32)
    ares_sb = vB(32, DC * 512, F32)
    ab16_sb = vC(0, 512, BF16)
    asq_sb = vC(1, 512, BF16)
    xm32_sb = [vC(2, 512, F32), vC(4, 512, F32)]
    xm16_sb = [vC(6, 512, BF16), vC(7, 512, BF16)]
    xmt_sb = [vC(8, 512, F32), vC(10, 512, F32)]
    bca_sb = vC(12, 512, F32)
    bcb_sb = vC(14, 512, F32)

    cwo = None
    for hh in range(NH):
        cwo = b.dma("gpsimd", wo_sb[:, hh * D:(hh + 1) * D],
                    wo_t[hh * E:(hh + 1) * E, :], "wld")
    b.wait("tensor", "wld", cwo)

    def ln_rows(stat_pe_milestone, neg=True):
        b.wait("scalar", "s_pe", stat_pe_milestone)
        b.wait("scalar", "s_ve", b.cnt["s_ve"])
        b.wait("scalar", "s_ac", b.cnt["s_ac"])
        if PH2_RCUT < 1:
            return b.op("scalar", lambda e: e.memzero(bca_sb[:, 0:4]), "s_ac")
        mean_m = b.op("scalar", lambda e: e.activation(
            r_mean[:, :], PF[2][0:1, :], AFCopy, scale=1.0 / D), pr=(2,))
        b.op("scalar", lambda e: e.activation(r_sq[:, :], PF[3][0:1, :], AFCopy, scale=1.0 / D), pr=(3,))
        b.wait("scalar", "s_ac", mean_m)
        if PH2_RCUT < 2:
            return b.op("scalar", lambda e: e.memzero(bca_sb[:, 0:4]), "s_ac")
        m2m = b.op("scalar", lambda e: e.activation(r_m2[:, :], r_mean[:, :], AFSquare), "s_ac")
        if PH2_RCUT < 3:
            return b.op("scalar", lambda e: e.memzero(bca_sb[:, 0:4]), "s_ac")
        b.wait("vector", "s_ac", m2m)
        b.op("vector", lambda e: e.tensor_sub(r_var[:, :], r_sq[:, :], r_m2[:, :]))
        b.wait("vector", "s_ve", b.cnt["s_ve"])
        vv = b.op("vector", lambda e: e.tensor_scalar(
            r_lnv[:, :], r_var[:, :], 1e-5, None, ALU.add), "s_ve")
        if PH2_RCUT < 4:
            b.wait("scalar", "s_ve", vv)
            return b.op("scalar", lambda e: e.memzero(bca_sb[:, 0:4]), "s_ac")
        b.wait("scalar", "s_ve", vv)
        lnv_m = b.op("scalar", lambda e: e.activation(r_var[:, :], r_lnv[:, :], AFLn))
        b.wait("scalar", "s_ac", lnv_m)
        rst = b.op("scalar", lambda e: e.activation(
            r_rstd[:, :], r_var[:, :], AFExp, scale=-0.5), "s_ac")
        if PH2_RCUT < 5:
            return rst
        b.wait("vector", "s_ac", rst)
        rbm = b.op("vector", lambda e: e.tensor_mul(r_b[:, :], r_mean[:, :], r_rstd[:, :]), "s_ve")
        if PH2_RCUT < 6:
            b.wait("scalar", "s_ve", rbm)
            return b.op("scalar", lambda e: e.memzero(bca_sb[:, 0:4]), "s_ac")
        b.wait("tensor", "s_ac", rst)
        b.wait("tensor", "s_ve", rbm)
        b.op("tensor", lambda e: e.matmul(
            PF[4][:, :], onesf_sb[0:1, :], r_rstd[:, :], start=True, stop=True,
            skip_group_check=True), None, pw=(4,))
        bcb = b.op("tensor", lambda e: e.matmul(
            PF[5][:, :], nonesf_sb[0:1, :], r_b[:, :], start=True, stop=True,
            skip_group_check=True), "s_pe", pw=(5,))
        if PH2_RCUT < 7:
            b.wait("scalar", "s_pe", bcb)
            return b.op("scalar", lambda e: e.memzero(bca_sb[:, 0:4]), "s_ac")
        b.wait("scalar", "s_pe", bcb)
        b.op("scalar", lambda e: e.activation(bca_sb[:, :], PF[4][:, :], AFCopy), pr=(4,))
        ev5 = b.op("scalar", lambda e: e.activation(bcb_sb[:, :], PF[5][:, :], AFCopy), "s_ac", pr=(5,))
        return ev5

    xm_dma32 = {}
    xm_dma16 = {}
    ph2_guard = {}
    prev_sc_pe = 0
    prev_sc_ve = 0
    for sc in range(PH2_NSC):
        b.wait("sync", "s_pe", prev_sc_pe)
        b.wait("sync", "s_ve", prev_sc_ve)
        czt = None
        for hh in range(NH):
            czt = b.dma("sync", zt_sb[:, hh * 512:(hh + 1) * 512],
                        z_scr[hh * P:(hh + 1) * P, sc * SCW:(sc + 1) * SCW], "p2l")
        cxq = b.dma("sync", xq_sb[:, :].rearrange("p (j s) -> p j s", s=SCW),
                    x_q[:, sc * SCW:(sc + 1) * SCW].rearrange("(j p) s -> p j s", p=P), "p2l")
        b.wait("tensor", "p2l", cxq)
        b.wait("tensor", "s_ve", prev_sc_ve)
        stat_mm = None
        for dc in range(DC):
            bank = dc % 2
            engn, mi = ph2_guard.get(bank, (None, 0))
            if engn:
                b.wait("tensor", engn, mi)
            mm = None
            for hh in range(NH):
                mm = b.op("tensor", lambda e, bank=bank, hh=hh, dc=dc: e.matmul(
                    PF[bank][:, :], wo_sb[:, hh * D + dc * E: hh * D + (dc + 1) * E],
                    zt_sb[:, hh * 512:(hh + 1) * 512],
                    start=(hh == 0), stop=(hh == NH - 1), skip_group_check=True),
                    "s_pe" if hh == NH - 1 else None, pw=(bank,))
            b.wait("vector", "s_pe", mm)
            ar = b.op("vector", lambda e, dc=dc, bank=bank: e.tensor_add(
                ares_sb[:, dc * 512:(dc + 1) * 512], PF[bank][:, :],
                xq_sb[:, dc * 512:(dc + 1) * 512]), "s_ve", pr=(bank,))
            ph2_guard[bank] = ("s_ve", ar)
            if PH2_PARTS == "ores":
                continue
            b.wait("vector", "s_ve", ar)
            cb = b.op("vector", lambda e, dc=dc: e.tensor_copy(
                ab16_sb[:, :], ares_sb[:, dc * 512:(dc + 1) * 512]), "s_ve")
            b.wait("scalar", "s_ve", cb)
            sq = b.op("scalar", lambda e, dc=dc: e.activation(
                asq_sb[:, :], ares_sb[:, dc * 512:(dc + 1) * 512], AFSquare), "s_ac")
            b.wait("tensor", "s_ve", cb)
            b.wait("tensor", "s_ac", sq)
            b.op("tensor", lambda e, dc=dc: e.matmul(
                PF[2][0:1, :], onesb_sb[:, 0:1], ab16_sb[:, :],
                start=(dc == 0), stop=(dc == DC - 1), skip_group_check=True), None, pw=(2,))
            stat_mm = b.op("tensor", lambda e, dc=dc: e.matmul(
                PF[3][0:1, :], onesb_sb[:, 0:1], asq_sb[:, :],
                start=(dc == 0), stop=(dc == DC - 1), skip_group_check=True),
                "s_pe", pw=(3,))
            if dc < DC - 1:
                b.wait("vector", "s_pe", stat_mm)
                b.wait("scalar", "s_pe", stat_mm)
        if PH2_PARTS in ("ores", "stats"):
            prev_sc_pe = b.cnt["s_pe"]
            prev_sc_ve = b.cnt["s_ve"]
            continue
        bev = ln_rows(stat_mm)
        if PH2_PARTS == "rows":
            prev_sc_pe = b.cnt["s_pe"]
            prev_sc_ve = b.cnt["s_ve"]
            continue
        b.wait("vector", "s_ac", bev)
        for dc in range(DC):
            gi = sc * DC + dc
            xb = dc % 2
            b.wait("vector", f"xm{xb}", xm_dma32.get(gi - 2, 0))
            t1 = b.op("vector", lambda e, dc=dc, xb=xb: e.tensor_mul(
                xmt_sb[xb][:, :], bca_sb[:, :], ares_sb[:, dc * 512:(dc + 1) * 512]), "s_ve")
            b.wait("vector", "s_ve", t1)
            t2 = b.op("vector", lambda e, xb=xb: e.tensor_add(
                xm32_sb[xb][:, :], bcb_sb[:, :], xmt_sb[xb][:, :]), "s_ve")
            if PH2_PARTS == "normnodma":
                continue
            b.wait("scalar", "s_ve", t2)
            b.wait("scalar", f"xm{xb}", xm_dma16.get(gi - 2, 0))
            c16 = b.op("scalar", lambda e, xb=xb: e.activation(
                xm16_sb[xb][:, :], xm32_sb[xb][:, :], AFCopy), "s_ac")
            if PH2_PARTS == "norm16":
                continue
            b.wait("sync", "s_ve", t2)
            xm_dma32[gi] = b.dma(
                "sync", xm32_scr[dc * P:(dc + 1) * P, sc * SCW:(sc + 1) * SCW],
                xm32_sb[xb][:, :], f"xm{xb}")
            b.wait("sync", "s_ac", c16)
            xm_dma16[gi] = b.dma(
                "sync", xmbf_scr[dc * P:(dc + 1) * P, sc * SCW:(sc + 1) * SCW],
                xm16_sb[xb][:, :], f"xm{xb}")
        prev_sc_pe = b.cnt["s_pe"]
        prev_sc_ve = b.cnt["s_ve"]

    b.wait_all_dma("sync")
    barrier()
    if NPHASES < 4:
        return finish()

    hid_sb = vA(0, 64 * 512, BF16)
    up_sb = [vA(64, DC * 512, BF16), vA(80, DC * 512, BF16)]
    xm16f_sb = vB(0, DC * 512, BF16)
    res_sb = vB(16, DC * 512, F32)
    dn_sb = [vB(48, 4 * 512, BF16), vB(52, 4 * 512, BF16)]
    xmr_sb = [vB(56, 512, F32), vB(60, 512, F32)]
    ffn_sb = [vC(2, 512, F32), vC(4, 512, F32)]
    out_sb = [vC(6, 512, F32), vC(8, 512, F32)]

    up_loaded = [0, 0]
    dn_pe_after = [0, 0]
    ph3_guard = {}
    out_dma = {}
    rv_prev = [0, 0]
    prev_sc_pe3 = 0
    for sc in range(PH3_NSC):
        b.wait("sync", "s_pe", prev_sc_pe3)
        cxm = b.dma("sync", xm16f_sb[:, :].rearrange("p (j s) -> p j s", s=SCW),
                    xmbf_scr[:, sc * SCW:(sc + 1) * SCW].rearrange("(j p) s -> p j s", p=P),
                    "p3x")
        b.wait("tensor", "p3x", cxm)
        for hb_i in range(16):
            ub = hb_i % 2
            b.wait("sync", "s_pe", up_loaded[ub])
            cup = b.dma("sync", up_sb[ub][:, :].rearrange("p (dc h) -> p dc h", h=512),
                        up_scr[hb_i * D:(hb_i + 1) * D, :].rearrange("(dc p) h -> p dc h", p=P),
                        f"up{ub}")
            b.wait("tensor", f"up{ub}", cup)
            for ht in range(4):
                bank = ht % 2
                engn, mi = ph3_guard.get(bank, (None, 0))
                if engn:
                    b.wait("tensor", engn, mi)
                mm = None
                for dc in range(DC):
                    mm = b.op("tensor", lambda e, bank=bank, dc=dc, ht=ht, ub=ub: e.matmul(
                        PF[bank][:, :], up_sb[ub][:, dc * 512 + ht * P: dc * 512 + (ht + 1) * P],
                        xm16f_sb[:, dc * 512:(dc + 1) * 512],
                        start=(dc == 0), stop=(dc == DC - 1), skip_group_check=True),
                        "s_pe" if dc == DC - 1 else None, pw=(bank,))
                htg = hb_i * 4 + ht
                b.wait("scalar", "s_pe", mm)
                hv = b.op("scalar", lambda e, htg=htg, bank=bank: e.activation(
                    hid_sb[:, htg * 512:(htg + 1) * 512], PF[bank][:, :], AFRelu,
                    scale=upsc_sb[:, htg:htg + 1]), "s_ac", pr=(bank,))
                ph3_guard[bank] = ("s_ac", hv)
                last_hv = hv
            up_loaded[ub] = b.cnt["s_pe"]
        b.wait("tensor", "s_ac", last_hv)
        for dblk in (range(PH3_NDBLK) if PH3_PARTS in ("hiddown", "downmm", "downfv", "full") else []):
            mm_last = None
            for htq in range(PH3_NHTQ):
                db = htq % 2
                b.wait("sync", "s_pe", dn_pe_after[db])
                cdn = b.dma(
                    "sync", dn_sb[db][:, :].rearrange("p (t d) -> p t d", d=512),
                    down_scr[dblk * DH + htq * 512: dblk * DH + (htq + 1) * 512, :]
                    .rearrange("(t p) d -> p t d", p=P), f"dn{db}")
                b.wait("tensor", f"dn{db}", cdn)
                for tq in range(4):
                    ht64 = htq * 4 + tq
                    for dc4 in range(4):
                        mm_last = b.op("tensor", lambda e, dc4=dc4, tq=tq, db=db, ht64=ht64:
                                       e.matmul(
                            PF[2 + dc4][:, :],
                            dn_sb[db][:, tq * 512 + dc4 * P: tq * 512 + (dc4 + 1) * P],
                            hid_sb[:, ht64 * 512:(ht64 + 1) * 512],
                            start=(ht64 == 0), stop=(ht64 == PH3_NHTQ * 4 - 1), skip_group_check=True),
                            "s_pe" if (tq == 3 and dc4 == 3) else None,
                            pw=(2 + dc4,))
                dn_pe_after[db] = mm_last
            for dc4 in (range(4) if PH3_PARTS != "downmm" else []):
                dcg = dblk * 4 + dc4
                xb = dc4 % 2
                b.wait("scalar", "s_pe", mm_last)
                b.wait("scalar", "s_ve", rv_prev[xb])
                fv = b.op("scalar", lambda e, dc4=dc4, dcg=dcg, xb=xb: e.activation(
                    ffn_sb[xb][:, :], PF[2 + dc4][:, :], AFCopy,
                    scale=dseff_sb[:, dcg:dcg + 1]), "s_ac", pr=(2 + dc4,))
                fv_sem = "s_ac"
                if PH3_PARTS != "downfv":
                    b.wait("sync", "s_ve", rv_prev[xb])
                    cxr = b.dma("sync", xmr_sb[xb][:, :],
                                xm32_scr[dcg * P:(dcg + 1) * P, sc * SCW:(sc + 1) * SCW], f"xr{xb}")
                    b.wait("vector", fv_sem, fv)
                    b.wait("vector", f"xr{xb}", cxr)
                    rv = b.op("vector", lambda e, dcg=dcg, xb=xb: e.tensor_add(
                        res_sb[:, dcg * 512:(dcg + 1) * 512], ffn_sb[xb][:, :],
                        xmr_sb[xb][:, :]), "s_ve")
                    rv_prev[xb] = rv
                b.wait("tensor", fv_sem, fv)
        if PH3_PARTS != "full":
            continue
        stat_mm = None
        b.wait("vector", "s_ve", b.cnt["s_ve"])
        b.wait("scalar", "s_ve", b.cnt["s_ve"])
        for dc in range(DC):
            cb = b.op("vector", lambda e, dc=dc: e.tensor_copy(
                ab16_sb[:, :], res_sb[:, dc * 512:(dc + 1) * 512]), "s_ve")
            b.wait("scalar", "s_ve", cb)
            sq = b.op("scalar", lambda e, dc=dc: e.activation(
                asq_sb[:, :], res_sb[:, dc * 512:(dc + 1) * 512], AFSquare), "s_ac")
            b.wait("tensor", "s_ve", cb)
            b.wait("tensor", "s_ac", sq)
            b.op("tensor", lambda e, dc=dc: e.matmul(
                PF[2][0:1, :], onesb_sb[:, 0:1], ab16_sb[:, :],
                start=(dc == 0), stop=(dc == DC - 1), skip_group_check=True), None, pw=(2,))
            stat_mm = b.op("tensor", lambda e, dc=dc: e.matmul(
                PF[3][0:1, :], onesb_sb[:, 0:1], asq_sb[:, :],
                start=(dc == 0), stop=(dc == DC - 1), skip_group_check=True),
                "s_pe", pw=(3,))
            if dc < DC - 1:
                b.wait("vector", "s_pe", stat_mm)
                b.wait("scalar", "s_pe", stat_mm)
        bev = ln_rows(stat_mm)
        b.wait("vector", "s_ac", bev)
        for dc in range(DC):
            gi = sc * DC + dc
            xb = dc % 2
            b.wait("vector", f"ot{xb}", out_dma.get(gi - 2, 0))
            t1 = b.op("vector", lambda e, dc=dc, xb=xb: e.tensor_mul(
                ffn_sb[xb][:, :], bca_sb[:, :], res_sb[:, dc * 512:(dc + 1) * 512]), "s_ve")
            b.wait("vector", "s_ve", t1)
            t2 = b.op("vector", lambda e, xb=xb: e.tensor_add(
                out_sb[xb][:, :], bcb_sb[:, :], ffn_sb[xb][:, :]), "s_ve")
            b.wait("sync", "s_ve", t2)
            out_dma[gi] = b.dma(
                "sync", outT[dc * P:(dc + 1) * P, sc * SCW:(sc + 1) * SCW],
                out_sb[xb][:, :], f"ot{xb}")
        b.wait("tensor", "s_ve", b.cnt["s_ve"])
        prev_sc_pe3 = b.cnt["s_pe"]

    return finish()


def kernel(L0_attn_x, L0_attn_W_Q, L0_attn_W_K, L0_attn_W_V, L0_attn_W_O,
           L0_ffn_up_weight, L0_ffn_down_weight, L0_ffn_up_scale,
           L0_ffn_down_scale, L0_ffn_output_scale):
    x = np.asarray(L0_attn_x, dtype=np.float32)
    wq = np.asarray(L0_attn_W_Q, dtype=np.float32)
    wk = np.asarray(L0_attn_W_K, dtype=np.float32)
    wv = np.asarray(L0_attn_W_V, dtype=np.float32)
    wo = np.asarray(L0_attn_W_O, dtype=np.float32)
    upw = np.asarray(L0_ffn_up_weight, dtype=np.float32)
    dnw = np.asarray(L0_ffn_down_weight, dtype=np.float32)
    ups = np.asarray(L0_ffn_up_scale, dtype=np.float32)
    dns = np.asarray(L0_ffn_down_scale, dtype=np.float32)
    osc = np.asarray(L0_ffn_output_scale, dtype=np.float32)

    xT = np.ascontiguousarray(x.transpose(0, 2, 1))
    wq_t = np.ascontiguousarray(wq.transpose(0, 2, 1)).reshape(NH * D, E)
    wk_t = np.ascontiguousarray(wk.transpose(0, 2, 1)).reshape(NH * D, E)
    wv_t = np.ascontiguousarray(wv.transpose(0, 2, 1)).reshape(NH * D, E)
    wo_t = np.ascontiguousarray(wo.transpose(0, 2, 1)).reshape(NH * E, D)
    up_t = np.ascontiguousarray(upw.T)
    down_t = np.ascontiguousarray(dnw.T)
    upsc2d = np.ascontiguousarray(ups.reshape(64, P))
    dssc2d = np.ascontiguousarray(dns.reshape(16, P))
    osc2d = np.ascontiguousarray(osc.reshape(1, 1))

    eye_f32 = np.eye(P, dtype=np.float32)
    eye_bf = eye_f32.astype(ml_dtypes.bfloat16)
    ones_f32 = np.ones((P, P), dtype=np.float32)
    ones_bf = ones_f32.astype(ml_dtypes.bfloat16)
    nones_f32 = -ones_f32

    tri = np.triu(np.full((P, P), -1e9, dtype=np.float32), k=1)
    masks = []
    for par in range(2):
        m = np.zeros((P, 512), dtype=np.float32)
        if par == 0:
            m[:, 256:384] = tri
            m[:, 384:512] = -1e9
        else:
            m[:, 384:512] = tri
        masks.append(m)

    nc = build_program()

    in_maps = []
    for c in range(8):
        bb, par = c // 2, c % 2
        xb_T = xT[bb]
        x_q = np.ascontiguousarray(
            xb_T.reshape(D, 32, P)[:, par::2, :].reshape(D, SOWN))
        in_maps.append({
            "x_kv": np.ascontiguousarray(xb_T),
            "x_q": x_q,
            "wq_t": wq_t, "wk_t": wk_t, "wv_t": wv_t, "wo_t": wo_t,
            "up_t": up_t, "down_t": down_t,
            "upsc2d": upsc2d, "dssc2d": dssc2d, "osc": osc2d,
            "mask512": masks[par],
            "eye_bf": eye_bf, "eye_f32": eye_f32,
            "ones_bf": ones_bf, "ones_f32": ones_f32, "nones_f32": nones_f32,
        })

    res = run_bass_kernel_spmd(nc, in_maps, core_ids=list(range(8)))
    global LAST_RES
    LAST_RES = res

    out = np.empty((4, S, D), dtype=np.float32)
    for bb in range(4):
        full = np.empty((D, 32, P), dtype=np.float32)
        for par in range(2):
            shard = res.results[2 * bb + par]["outT"]
            full[:, par::2, :] = shard.reshape(D, NSLOT, P)
        out[bb] = full.reshape(D, S).T
    return out


# version 03
# speedup vs baseline: 1.2196x; 1.1378x over previous
import math
import numpy as np
import ml_dtypes
import concourse.bass as bass
import concourse.mybir as mybir
from concourse.bass_utils import run_bass_kernel_spmd

F32 = mybir.dt.float32
BF16 = mybir.dt.bfloat16
AF = mybir.ActivationFunctionType
ALU = mybir.AluOpType
AX = mybir.AxisListType

P = 128
D = 2048
DC = 16
NH = 16
E = 128
S = 4096
SOWN = 2048
DH = 8192
NSLOT = 16
SCW = 512
QSCALE = 1.0 / math.sqrt(128.0)

ENGS = ("sync", "scalar", "vector", "tensor", "gpsimd")
OCT_SLOTS = [[15, 13, 11, 9, 7, 5, 3, 1], [16, 14, 12, 10, 8, 6, 4, 2]]


def _slot_off(oct_slots):
    offs = {}
    o = 0
    for s in oct_slots:
        offs[s] = o
        o += 256 * s
    return offs


class Builder:
    def __init__(self):
        self.nc = bass.Bass("TRN2", target_bir_lowering=False)
        self.q = {e: [] for e in ENGS}
        self.cnt = {}
        self.sem = {}
        self.dma_sems = set()
        self.log = []

    def new_sem(self, name):
        self.sem[name] = self.nc.alloc_semaphore(name=name)
        self.cnt[name] = 0

    def emit(self, eng, f):
        self.q[eng].append(f)

    def dma(self, eng, dst, src, semname, **kw):
        if semname not in self.sem:
            self.new_sem(semname)
        self.dma_sems.add(semname)
        self.cnt[semname] += 16
        sem = self.sem[semname]

        def f(e, dst=dst, src=src, kw=kw, sem=sem):
            e.dma_start(dst, src, **kw).then_inc(sem, 16)
        self.emit(eng, f)
        self.log.append(("dma", eng, semname, self.cnt[semname], (), ()))
        return self.cnt[semname]

    def wait(self, eng, semname, val):
        if val is None or val <= 0:
            return
        assert val <= self.cnt[semname], (semname, val, self.cnt[semname])
        sem = self.sem[semname]

        def f(e, sem=sem, val=val):
            e.wait_ge(sem, val)
        self.emit(eng, f)
        self.log.append(("wait", eng, semname, val, (), ()))

    def wait_all_dma(self, eng):
        for s in sorted(self.dma_sems):
            self.wait(eng, s, self.cnt[s])

    ENGSEM = {"tensor": "s_pe", "scalar": "s_ac", "vector": "s_ve", "gpsimd": "s_po"}

    def op(self, eng, f, semname=None, pr=(), pw=()):
        self.log.append(("op", eng, semname,
                         self.cnt[semname] + 1 if semname else None, pr, pw))
        if semname is None:
            self.emit(eng, f)
            return None
        self.cnt[semname] += 1
        sem = self.sem[semname]

        def g(e, f=f, sem=sem):
            f(e).then_inc(sem, 1)
        self.emit(eng, g)
        return self.cnt[semname]


NPHASES = 4
PH1_MODE = "full"
PH3_PARTS = "full"
PH2_PARTS = "full"
PH2_RCUT = 9
PH2_NSC = 4
PH3_NDBLK = 4
PH3_NHTQ = 16
PH3_NSC = 4
PH3_FV = "act"
LAST_RES = None
TAPS = ()


def build_program():
    b = Builder()
    nc = b.nc

    x_kv = nc.dram_tensor("x_kv", [D, S], F32, kind="ExternalInput").ap()
    x_q = nc.dram_tensor("x_q", [D, SOWN], F32, kind="ExternalInput").ap()
    wq_t = nc.dram_tensor("wq_t", [NH * D, E], F32, kind="ExternalInput").ap()
    wk_t = nc.dram_tensor("wk_t", [NH * D, E], F32, kind="ExternalInput").ap()
    wv_t = nc.dram_tensor("wv_t", [NH * D, E], F32, kind="ExternalInput").ap()
    wo_t = nc.dram_tensor("wo_t", [NH * E, D], F32, kind="ExternalInput").ap()
    up_t = nc.dram_tensor("up_t", [D, DH], F32, kind="ExternalInput").ap()
    down_t = nc.dram_tensor("down_t", [DH, D], F32, kind="ExternalInput").ap()
    upsc2d = nc.dram_tensor("upsc2d", [64, P], F32, kind="ExternalInput").ap()
    dssc2d = nc.dram_tensor("dssc2d", [16, P], F32, kind="ExternalInput").ap()
    osc = nc.dram_tensor("osc", [1, 1], F32, kind="ExternalInput").ap()
    mask512 = nc.dram_tensor("mask512", [P, 512], F32, kind="ExternalInput").ap()
    eye_bf = nc.dram_tensor("eye_bf", [P, P], BF16, kind="ExternalInput").ap()
    eye_f32 = nc.dram_tensor("eye_f32", [P, P], F32, kind="ExternalInput").ap()
    ones_bf = nc.dram_tensor("ones_bf", [P, P], BF16, kind="ExternalInput").ap()
    ones_f32 = nc.dram_tensor("ones_f32", [P, P], F32, kind="ExternalInput").ap()
    nones_f32 = nc.dram_tensor("nones_f32", [P, P], F32, kind="ExternalInput").ap()
    outT = nc.dram_tensor("outT", [D, SOWN], F32, kind="ExternalOutput").ap()

    k_scr = nc.dram_tensor("k_scr", [NH * P, S], BF16, kind="Internal").ap()
    v_scr = nc.dram_tensor("v_scr", [NH * P, S], BF16, kind="Internal").ap()
    q_scr = nc.dram_tensor("q_scr", [NH * P, SOWN], BF16, kind="Internal").ap()
    z_scr = nc.dram_tensor("z_scr", [NH * P, SOWN], BF16, kind="Internal").ap()
    xm32_scr = nc.dram_tensor("xm32_scr", [DC * P, SOWN], F32, kind="Internal").ap()
    xmbf_scr = nc.dram_tensor("xmbf_scr", [DC * P, SOWN], BF16, kind="Internal").ap()
    up_scr = nc.dram_tensor("up_scr", [16 * D, 512], BF16, kind="Internal").ap()
    down_scr = nc.dram_tensor("down_scr", [4 * DH, 512], BF16, kind="Internal").ap()

    eyeb_sb = nc.alloc_sbuf_tensor("eyeb_sb", [P, P], BF16)
    eyef_sb = nc.alloc_sbuf_tensor("eyef_sb", [P, P], F32)
    onesb_sb = nc.alloc_sbuf_tensor("onesb_sb", [P, P], BF16)
    onesf_sb = nc.alloc_sbuf_tensor("onesf_sb", [P, P], F32)
    nonesf_sb = nc.alloc_sbuf_tensor("nonesf_sb", [P, P], F32)
    mask_sb = nc.alloc_sbuf_tensor("mask_sb", [P, 512], F32)
    upsc_sb = nc.alloc_sbuf_tensor("upsc_sb", [P, 64], F32)
    dseff_sb = nc.alloc_sbuf_tensor("dseff_sb", [P, 16], F32)
    scst_sb = nc.alloc_sbuf_tensor("scst_sb", [64, P], F32)
    osb_sb = nc.alloc_sbuf_tensor("osb_sb", [P, 1], F32)
    negmax_sb = nc.alloc_sbuf_tensor("negmax_sb", [P, 16], F32)
    sums_sb = nc.alloc_sbuf_tensor("sums_sb", [P, 16], F32)
    lnsum_sb = nc.alloc_sbuf_tensor("lnsum_sb", [P, 16], F32)
    recip_sb = nc.alloc_sbuf_tensor("recip_sb", [P, 16], F32)
    r_sum = nc.alloc_sbuf_tensor("r_sum", [1, 512], F32)
    r_sq = nc.alloc_sbuf_tensor("r_sq", [1, 512], F32)
    r_mean = nc.alloc_sbuf_tensor("r_mean", [1, 512], F32)
    r_m2 = nc.alloc_sbuf_tensor("r_m2", [1, 512], F32)
    r_var = nc.alloc_sbuf_tensor("r_var", [1, 512], F32)
    r_lnv = nc.alloc_sbuf_tensor("r_lnv", [1, 512], F32)
    r_rstd = nc.alloc_sbuf_tensor("r_rstd", [1, 512], F32)
    r_b = nc.alloc_sbuf_tensor("r_b", [1, 512], F32)
    bar_a = nc.alloc_sbuf_tensor("bar_a", [1, 64], F32)
    bar_v = nc.alloc_sbuf_tensor("bar_v", [1, 64], F32)
    bar_g = nc.alloc_sbuf_tensor("bar_g", [1, 64], F32)

    AR_A = nc.alloc_sbuf_tensor("AR_A", [P, 49152], BF16).ap()
    AR_B = nc.alloc_sbuf_tensor("AR_B", [P, 32768], BF16).ap()
    AR_C = nc.alloc_sbuf_tensor("AR_C", [P, 8192], BF16).ap()

    def _view(ar, off_kb, n_elem, dtype):
        o = off_kb * 512
        if dtype == BF16:
            return ar[:, o:o + n_elem]
        return ar[:, o:o + 2 * n_elem].bitcast(F32)

    def vA(off_kb, n, dt):
        return _view(AR_A, off_kb, n, dt)

    def vB(off_kb, n, dt):
        return _view(AR_B, off_kb, n, dt)

    def vC(off_kb, n, dt):
        return _view(AR_C, off_kb, n, dt)

    PF = [nc.alloc_psum_tensor(f"PF{i}", [P, 512], F32) for i in range(6)]
    PTF = [nc.alloc_psum_tensor(f"PTF{i}", [P, 512], F32) for i in range(2)]

    for s in ["s_pe", "s_ac", "s_ve", "s_po", "bar"]:
        b.new_sem(s)

    scratch_map = {"k_scr": (k_scr, [NH * P, S], BF16),
                   "v_scr": (v_scr, [NH * P, S], BF16),
                   "q_scr": (q_scr, [NH * P, SOWN], BF16),
                   "z_scr": (z_scr, [NH * P, SOWN], BF16),
                   "xm32_scr": (xm32_scr, [DC * P, SOWN], F32),
                   "up_scr": (up_scr, [16 * D, 512], BF16),
                   "down_scr": (down_scr, [4 * DH, 512], BF16)}

    def finish():
        for name in TAPS:
            src, shape, dt = scratch_map[name]
            tap = nc.dram_tensor("tap_" + name, shape, dt, kind="ExternalOutput").ap()
            nrow = shape[0]
            for r0 in range(0, nrow, 1024):
                b.dma("sync", tap[r0:r0 + 1024, :], src[r0:r0 + 1024, :], "tap")
        b.wait_all_dma("sync")
        with nc.Block() as blk:
            @blk.sync
            def _(e):
                for f in b.q["sync"]:
                    f(e)

            @blk.scalar
            def _(e):
                for f in b.q["scalar"]:
                    f(e)

            @blk.vector
            def _(e):
                for f in b.q["vector"]:
                    f(e)

            @blk.tensor
            def _(e):
                for f in b.q["tensor"]:
                    f(e)

            @blk.gpsimd
            def _(e):
                for f in b.q["gpsimd"]:
                    f(e)
        return nc

    bar_idx = [0]

    def barrier():
        o = 8 * bar_idx[0]
        bar_idx[0] += 1
        b.op("tensor", lambda e: e.matmul(
            PF[0][0:1, 0:1], onesb_sb[0:1, 0:1], onesb_sb[0:1, 0:1],
            start=True, stop=True, skip_group_check=True), "s_pe", pw=(0,))
        b.op("tensor", lambda e: e.matmul(
            PF[0][0:1, 0:1], onesb_sb[0:1, 0:1], onesb_sb[0:1, 0:1],
            start=True, stop=True, skip_group_check=True), "bar", pw=(0,))
        b.op("scalar", lambda e, o=o: e.memzero(bar_a[:, o:o + 4]), "s_ac")
        b.op("scalar", lambda e, o=o: e.memzero(bar_a[:, o + 4:o + 8]), "bar")
        b.op("vector", lambda e, o=o: e.memset(bar_v[:, o:o + 4], 0.0), "s_ve")
        b.op("vector", lambda e, o=o: e.memset(bar_v[:, o + 4:o + 8], 0.0), "bar")
        b.op("gpsimd", lambda e, o=o: e.memset(bar_g[:, o:o + 4], 0.0), "s_po")
        b.op("gpsimd", lambda e, o=o: e.memset(bar_g[:, o + 4:o + 8], 0.0), "bar")
        b.cnt["bar"] += 1
        sem = b.sem["bar"]
        b.emit("sync", lambda e, sem=sem: e.nop().then_inc(sem, 1))
        b.log.append(("op", "sync", "bar", b.cnt["bar"], (), ()))
        tgt = b.cnt["bar"]
        for eng in ENGS:
            b.wait(eng, "bar", tgt)

    AFCopy, AFExp, AFLn, AFRelu, AFSquare = AF.Copy, AF.Exp, AF.Ln, AF.Relu, AF.Square

    b.dma("sync", eyeb_sb[:, :], eye_bf, "ic")
    b.dma("sync", eyef_sb[:, :], eye_f32, "ic")
    b.dma("sync", onesb_sb[:, :], ones_bf, "ic")
    b.dma("sync", onesf_sb[:, :], ones_f32, "ic")
    b.dma("sync", nonesf_sb[:, :], nones_f32, "ic")
    b.dma("sync", mask_sb[:, :], mask512, "ic")
    b.dma("sync", scst_sb[0:64, :], upsc2d, "ic")
    c_init = b.dma("sync", osb_sb[0:1, 0:1], osc, "ic")

    b.wait("tensor", "ic", c_init)
    b.op("tensor", lambda e: e.matmul(PF[0][:, 0:64], scst_sb[0:64, :],
                                      eyef_sb[0:64, 0:64], start=True, stop=True), pw=(0,))
    m1 = b.op("tensor", lambda e: e.matmul(
        PF[1][:, 0:1], onesf_sb[0:1, :], osb_sb[0:1, 0:1], start=True, stop=True), "s_pe", pw=(1,))
    b.wait("vector", "s_pe", m1)
    b.op("vector", lambda e: e.tensor_copy(upsc_sb[:, :], PF[0][:, 0:64]), pr=(0,))
    v1 = b.op("vector", lambda e: e.tensor_copy(osb_sb[:, 0:1], PF[1][:, 0:1]), "s_ve", pr=(1,))
    b.wait("sync", "s_pe", m1)
    c_ds = b.dma("sync", scst_sb[0:16, :], dssc2d, "ic")
    b.wait("tensor", "ic", c_ds)
    b.wait("tensor", "s_ve", v1)
    m2 = b.op("tensor", lambda e: e.matmul(
        PF[0][:, 0:16], scst_sb[0:16, :], eyef_sb[0:16, 0:16],
        start=True, stop=True), "s_pe", pw=(0,))
    b.wait("vector", "s_pe", m2)
    v_dseff = b.op("vector", lambda e: e.tensor_scalar(
        dseff_sb[:, :], PF[0][:, 0:16], osb_sb[:, 0:1], None, ALU.mult), "s_ve", pr=(0,))

    wq_sb = vA(0, 8 * D, BF16)
    wk_sb = vA(32, 8 * D, BF16)
    wv_sb = vA(64, 8 * D, BF16)
    xk_sb = [vB(0, DC * SCW, BF16), vB(16, DC * SCW, BF16)]
    ek_sb = [vB(32, 512, BF16), vB(33, 512, BF16)]
    ev_sb = [vB(34, 512, BF16), vB(35, 512, BF16)]
    eq_sb = [vB(36, 512, BF16), vB(37, 512, BF16)]
    sgf_sb = [vB(40, D, F32), vB(48, D, F32)]
    sgb_sb = [vB(56, D, BF16), vB(60, D, BF16)]

    sign_units = [("up", rt, cc) for rt in range(16) for cc in range(4)] + \
                 [("dn", rt, 0) for rt in range(64)]
    sign_i = 0
    sign_loaded = {}
    sign_signed = {}
    sign_dmas = {}

    def emit_sign_unit(idx):
        kind, rt, cc = sign_units[idx]
        buf = idx % 2
        if (idx - 2) in sign_signed:
            b.wait("sync", "s_ac", sign_signed[idx - 2])
        if kind == "up":
            src = up_t[rt * P:(rt + 1) * P, cc * D:(cc + 1) * D]
        else:
            src = down_t[rt * P:(rt + 1) * P, :]
        sign_loaded[idx] = b.dma("sync", sgf_sb[buf][:, :], src, f"sg{buf}")
        b.wait("scalar", f"sg{buf}", sign_loaded[idx])
        if (idx - 2) in sign_dmas:
            b.wait("scalar", f"sw{buf}", sign_dmas[idx - 2])
        sign_signed[idx] = b.op(
            "scalar", lambda e, buf=buf: e.sign(sgb_sb[buf][:, :], sgf_sb[buf][:, :]), "s_ac")
        b.wait("scalar", "s_ac", sign_signed[idx])
        for qq in range(4):
            if kind == "up":
                blk = 4 * cc + qq
                dst = up_scr[blk * D + rt * P: blk * D + (rt + 1) * P, :]
            else:
                dst = down_scr[qq * DH + rt * P: qq * DH + (rt + 1) * P, :]
            sign_dmas[idx] = b.dma(
                "scalar", dst, sgb_sb[buf][:, qq * 512:(qq + 1) * 512], f"sw{buf}")

    ek_last = [0, 0]
    ev_last = [0, 0]
    eq_last = [0, 0]
    pe_after_xuse = {}
    ph0_guard = {0: ("s_ve", 0), 1: ("s_ve", 0)}

    for hg in range(2):
        if hg > 0:
            b.wait("gpsimd", "s_pe", b.cnt["s_pe"])
        cw = None
        for wsb, wsrc in ((wq_sb, wq_t), (wk_sb, wk_t), (wv_sb, wv_t)):
            for hh in range(8):
                h = hg * 8 + hh
                src = wsrc[h * D:(h + 1) * D, :].rearrange("(dc p) e -> p dc e", p=P)
                dst = wsb[:, hh * D:(hh + 1) * D].rearrange("p (dc e) -> p dc e", e=E)
                cw = b.dma("gpsimd", dst, src, "wld")
        b.wait("tensor", "wld", cw)
        if hg == 0:
            ph0_guard[0] = ("s_ve", v_dseff)
            ph0_guard[1] = ("s_ve", v_dseff)
        for kind, n_chunks, xsrc in (("kv", 8, x_kv), ("q", 4, x_q)):
            for ci in range(n_chunks):
                xb = ci % 2
                prevkey = (hg, kind, ci - 2) if ci >= 2 else (hg, "kv", 6 + ci)
                if prevkey in pe_after_xuse:
                    b.wait("gpsimd", "s_pe", pe_after_xuse[prevkey])
                elif hg == 1:
                    b.wait("gpsimd", "s_pe", b.cnt["s_pe"])
                src = xsrc[:, ci * SCW:(ci + 1) * SCW].rearrange("(j p) s -> p j s", p=P)
                dst = xk_sb[xb][:, :].rearrange("p (j s) -> p j s", s=SCW)
                cx = b.dma("gpsimd", dst, src, f"x{xb}")
                b.wait("tensor", f"x{xb}", cx)
                for hh in range(8):
                    h = hg * 8 + hh
                    if kind == "kv":
                        specs = (("k", wk_sb, ek_sb, ek_last, k_scr, 0, "scalar", 1.0),
                                 ("v", wv_sb, ev_sb, ev_last, v_scr, 2, "vector", 1.0))
                    else:
                        specs = (("qq", wq_sb, eq_sb, eq_last, q_scr, 4, "scalar", QSCALE),)
                    for nm, wsb2, esb, elast, scr, bank0, evac_eng, scl in specs:
                        bidx0 = bank0 + (ci + hh) % 2
                        bank = PF[bidx0]
                        engn, mi = ph0_guard.get(bidx0, (None, 0))
                        if engn:
                            b.wait("tensor", engn, mi)
                        mm = None
                        for dc in range(DC):
                            lhs = wsb2[:, hh * D + dc * E: hh * D + (dc + 1) * E]
                            rhs = xk_sb[xb][:, dc * SCW:(dc + 1) * SCW]
                            mm = b.op("tensor", lambda e, bank=bank, lhs=lhs, rhs=rhs,
                                      st=(dc == 0), sp=(dc == DC - 1): e.matmul(
                                bank[:, :], lhs, rhs, start=st, stop=sp),
                                "s_pe" if dc == DC - 1 else None,
                                pw=(bank0 + (ci + hh) % 2,))
                        eb = ci % 2
                        b.wait(evac_eng, "s_pe", mm)
                        b.wait(evac_eng, f"eo_{nm}{eb}", elast[eb])
                        if evac_eng == "scalar":
                            ev = b.op("scalar", lambda e, esb=esb, eb=eb, bank=bank, scl=scl:
                                      e.activation(esb[eb][:, :], bank[:, :], AFCopy, scale=scl),
                                      "s_ac", pr=(bank0 + (ci + hh) % 2,))
                            b.wait("sync", "s_ac", ev)
                            ph0_guard[bidx0] = ("s_ac", ev)
                        else:
                            ev = b.op("vector", lambda e, esb=esb, eb=eb, bank=bank:
                                      e.tensor_copy(esb[eb][:, :], bank[:, :]), "s_ve",
                                      pr=(bank0 + (ci + hh) % 2,))
                            b.wait("sync", "s_ve", ev)
                            ph0_guard[bidx0] = ("s_ve", ev)
                        elast[eb] = b.dma(
                            "sync", scr[h * P:(h + 1) * P, ci * SCW:(ci + 1) * SCW],
                            esb[eb][:, :], f"eo_{nm}{eb}")
                pe_after_xuse[(hg, kind, ci)] = b.cnt["s_pe"]
                if sign_i < len(sign_units):
                    emit_sign_unit(sign_i)
                    sign_i += 1
    while sign_i < len(sign_units):
        emit_sign_unit(sign_i)
        sign_i += 1

    b.wait_all_dma("sync")
    barrier()
    if NPHASES < 2:
        return finish()

    K_sb = [vA(0, S, BF16), vA(8, S, BF16)]
    V_sb = [vA(16, S, BF16), vA(24, S, BF16)]
    Q_sb = [vA(32, SOWN, BF16), vA(36, SOWN, BF16)]
    sc_sb = [vA(40, S, F32), vA(56, S, F32)]
    pt_sb = [vA(72, 512, BF16), vA(73, 512, BF16), vA(74, 512, BF16), vA(75, 512, BF16)]
    zq_sb = [vA(76, 512, BF16), vA(78, 512, BF16)]
    P_oct = vB(0, 18432, BF16)

    kq_loaded = [0, 0]
    v_loaded = [0, 0]
    head_done = {}

    def load_head(h):
        hb = h % 2
        if h >= 2:
            b.wait("sync", "s_pe", head_done[h - 2])
        b.dma("sync", K_sb[hb][:, :], k_scr[h * P:(h + 1) * P, :], f"kq{hb}")
        kq_loaded[hb] = b.dma("sync", Q_sb[hb][:, :], q_scr[h * P:(h + 1) * P, :], f"kq{hb}")
        if h >= 2:
            b.wait("scalar", "s_pe", head_done[h - 2])
        if PH1_MODE == "plainV":
            v_loaded[hb] = b.dma("scalar", V_sb[hb][:, :],
                                 v_scr[h * P:(h + 1) * P, :], f"v{hb}")
        else:
            for c in range(32):
                v_loaded[hb] = b.dma(
                    "scalar", V_sb[hb][:, c * E:(c + 1) * E],
                    v_scr[h * P:(h + 1) * P, c * E:(c + 1) * E], f"v{hb}", transpose=True)

    load_head(0)
    load_head(1)

    bank_last_evac = {0: ("s_ve", 0), 1: ("s_ve", 0)}
    t_bank_last_evac = {0: ("s_ve", 0), 1: ("s_ve", 0)}
    av_ms = {}
    z_bank_prev = {i: ("s_ve", 0) for i in range(4)}
    zq_last = [0, 0]
    prev_B_pe = 0

    for h in range(NH):
        hb = h % 2
        b.wait("tensor", f"kq{hb}", kq_loaded[hb])
        for oi, oct_slots in enumerate(OCT_SLOTS):
            offs = _slot_off(oct_slots)
            norm_done = {}
            first_slot = True
            for i in sorted(oct_slots):
                R = 256 * i
                ngrp = (R + 511) // 512
                icol = i - 1
                po = offs[i]
                grp_ev = []
                for g in range(ngrp):
                    w = min(512, R - g * 512)
                    bank = g % 2
                    engn, mi = bank_last_evac[bank]
                    b.wait("tensor", engn, mi)
                    mm = b.op("tensor", lambda e, bank=bank, hb=hb, i=i, g=g, w=w: e.matmul(
                        PF[bank][:, 0:w], Q_sb[hb][:, (i - 1) * P:i * P],
                        K_sb[hb][:, g * 512:g * 512 + w], start=True, stop=True,
                        skip_group_check=True), "s_pe", pw=(bank,))
                    last = (g == ngrp - 1)
                    if last:
                        b.wait("vector", "s_pe", mm)
                        mk = mask_sb[:, 0:512] if w == 512 else mask_sb[:, 256:512]
                        ev = b.op("vector", lambda e, g=g, w=w, bank=bank, mk=mk, hb=hb:
                                  e.tensor_add(sc_sb[hb][:, g * 512:g * 512 + w],
                                               PF[bank][:, 0:w], mk), "s_ve", pr=(bank,))
                        bank_last_evac[bank] = ("s_ve", ev)
                        grp_ev.append(("s_ve", ev))
                    elif g % 2 == 0:
                        b.wait("scalar", "s_pe", mm)
                        ev = b.op("scalar", lambda e, g=g, w=w, bank=bank, hb=hb:
                                  e.activation(sc_sb[hb][:, g * 512:g * 512 + w],
                                               PF[bank][:, 0:w], AFCopy), "s_ac", pr=(bank,))
                        bank_last_evac[bank] = ("s_ac", ev)
                        grp_ev.append(("s_ac", ev))
                    else:
                        b.wait("vector", "s_pe", mm)
                        ev = b.op("vector", lambda e, g=g, w=w, bank=bank, hb=hb:
                                  e.tensor_copy(sc_sb[hb][:, g * 512:g * 512 + w],
                                                PF[bank][:, 0:w]), "s_ve", pr=(bank,))
                        bank_last_evac[bank] = ("s_ve", ev)
                        grp_ev.append(("s_ve", ev))
                for engn, ev in grp_ev:
                    if engn == "s_ac":
                        b.wait("vector", "s_ac", ev)
                b.wait("vector", "s_ve", b.cnt["s_ve"])
                mx = b.op("vector", lambda e, R=R, icol=icol, hb=hb: e.reduce_max(
                    negmax_sb[:, icol:icol + 1], sc_sb[hb][:, 0:R],
                    axis=AX.X, negate=True), "s_ve")
                b.wait("scalar", "s_ve", mx)
                if first_slot:
                    b.wait("scalar", "s_pe", prev_B_pe)
                    first_slot = False
                b.wait("scalar", "s_ac", b.cnt["s_ac"])
                exp_m = b.op("scalar", lambda e, R=R, icol=icol, po=po, hb=hb: e.activation(
                    P_oct[:, po:po + R], sc_sb[hb][:, 0:R], AFExp,
                    bias=negmax_sb[:, icol:icol + 1],
                    accum_out=sums_sb[:, icol:icol + 1]))
                b.wait("scalar", "s_ac", exp_m)
                ln_m = b.op("scalar", lambda e, icol=icol: e.activation(
                    lnsum_sb[:, icol:icol + 1], sums_sb[:, icol:icol + 1], AFLn))
                b.wait("scalar", "s_ac", ln_m)
                rc = b.op("scalar", lambda e, icol=icol: e.activation(
                    recip_sb[:, icol:icol + 1], lnsum_sb[:, icol:icol + 1], AFExp,
                    scale=-1.0), "s_ac")
                b.wait("vector", "s_ac", rc)
                nm = b.op("vector", lambda e, R=R, po=po, icol=icol: e.tensor_scalar(
                    P_oct[:, po:po + R], P_oct[:, po:po + R],
                    recip_sb[:, icol:icol + 1], None, ALU.mult), "s_ve")
                norm_done[i] = nm
            if PH1_MODE == "noB":
                prev_B_pe = b.cnt["s_pe"]
                continue
            b.wait("tensor", "s_ve", max(norm_done.values()))
            b.wait("tensor", f"v{hb}", v_loaded[hb])
            imax = oct_slots[0]
            quads = [oct_slots[0:4], oct_slots[4:8]]
            zbank = [PF[2 + 2 * oi], PF[3 + 2 * oi]]
            for qi in range(2):
                engn, mi = z_bank_prev[2 * oi + qi]
                b.wait("tensor", engn, mi)
            av_done = {}
            for qi, quad in enumerate(quads):
                pend = None
                def emit_av(c, nvq, evsem, ev, qi=None, quad=None):
                    b.wait("tensor", evsem, ev)
                    mm = b.op("tensor", lambda e, zb=zbank[qi], pb=2 * qi + c % 2, nvq=nvq,
                              st=(c == 0), sp=(c == 2 * quad[0] - 1), c=c, hb=hb: e.matmul(
                        zb[:, 0:nvq * P], V_sb[hb][:, c * E:(c + 1) * E],
                        pt_sb[pb][:, 0:nvq * P], start=st, stop=sp,
                        skip_group_check=True), "s_pe", pw=(2 + 2 * oi + qi,))
                    av_ms[(2 * oi + qi, c % 2)] = mm
                    if c == 2 * quad[0] - 1:
                        av_done[qi] = mm
                for c in range(2 * quad[0]):
                    nvq = len([sl for sl in quad if 2 * sl > c])
                    pb = 2 * qi + c % 2
                    tb = c % 2
                    engn, mi = t_bank_last_evac[tb]
                    b.wait("tensor", engn, mi)
                    tr = None
                    for pos in range(nvq):
                        sl = quad[pos]
                        poff = offs[sl] + c * P
                        if PH1_MODE != "noT":
                            tr = b.op("tensor", lambda e, tb=tb, pos=pos, poff=poff: e.matmul(
                                PTF[tb][:, pos * P:(pos + 1) * P],
                                P_oct[:, poff:poff + P], eyeb_sb[:, :],
                                start=True, stop=True, skip_group_check=True),
                                "s_pe" if pos == nvq - 1 else None, pw=(6 + tb,))
                    if tr is None:
                        tr = b.op("tensor", lambda e: e.nop(), "s_pe")
                    if (c + qi) % 2 == 0:
                        b.wait("scalar", "s_pe", tr)
                        b.wait("scalar", "s_pe", av_ms.get((2 * oi + qi, c % 2), 0))
                        ev = b.op("scalar", lambda e, pb=pb, tb=tb, nvq=nvq: e.activation(
                            pt_sb[pb][:, 0:nvq * P], PTF[tb][:, 0:nvq * P], AFCopy), "s_ac",
                            pr=(6 + tb,))
                        t_bank_last_evac[tb] = ("s_ac", ev)
                        evsem = "s_ac"
                    else:
                        b.wait("vector", "s_pe", tr)
                        b.wait("vector", "s_pe", av_ms.get((2 * oi + qi, c % 2), 0))
                        ev = b.op("vector", lambda e, pb=pb, tb=tb, nvq=nvq: e.tensor_copy(
                            pt_sb[pb][:, 0:nvq * P], PTF[tb][:, 0:nvq * P]), "s_ve",
                            pr=(6 + tb,))
                        t_bank_last_evac[tb] = ("s_ve", ev)
                        evsem = "s_ve"
                    if pend is not None:
                        emit_av(*pend, qi=qi, quad=quad)
                    pend = (c, nvq, evsem, ev)
                if pend is not None:
                    emit_av(*pend, qi=qi, quad=quad)
            for qi, quad in enumerate(quads):
                b.wait("vector", "s_pe", av_done[qi])
                b.wait("vector", f"z{qi}", zq_last[qi])
                zev = b.op("vector", lambda e, zb=zbank[qi], qi=qi: e.tensor_copy(
                    zq_sb[qi][:, :], zb[:, :]), "s_ve", pr=(2 + 2 * oi + qi,))
                z_bank_prev[2 * oi + qi] = ("s_ve", zev)
                b.wait("gpsimd", "s_ve", zev)
                for pos, sl in enumerate(quad):
                    zq_last[qi] = b.dma(
                        "gpsimd", z_scr[h * P:(h + 1) * P, (sl - 1) * P:sl * P],
                        zq_sb[qi][:, pos * P:(pos + 1) * P], f"z{qi}")
            prev_B_pe = b.cnt["s_pe"]
        head_done[h] = b.cnt["s_pe"]
        if h + 2 < NH:
            load_head(h + 2)

    b.wait_all_dma("sync")
    barrier()
    if NPHASES < 3:
        return finish()

    wo_sb = vA(0, NH * D, BF16)
    zt_sb = vA(64, NH * 512, BF16)
    xq_sb = vB(0, DC * 512, F32)
    ares_sb = vB(32, DC * 512, F32)
    ab16_sb = vC(0, 512, BF16)
    asq_sb = vC(1, 512, BF16)
    xm32_sb = [vC(2, 512, F32), vC(4, 512, F32)]
    xm16_sb = [vC(6, 512, BF16), vC(7, 512, BF16)]
    xmt_sb = [vC(8, 512, F32), vC(10, 512, F32)]
    bca_sb = vC(12, 512, F32)
    bcb_sb = vC(14, 512, F32)

    cwo = None
    for hh in range(NH):
        cwo = b.dma("gpsimd", wo_sb[:, hh * D:(hh + 1) * D],
                    wo_t[hh * E:(hh + 1) * E, :], "wld")
    b.wait("tensor", "wld", cwo)

    def ln_rows(stat_pe_milestone, neg=True):
        b.wait("scalar", "s_pe", stat_pe_milestone)
        b.wait("scalar", "s_ve", b.cnt["s_ve"])
        b.wait("scalar", "s_ac", b.cnt["s_ac"])
        if PH2_RCUT < 1:
            return b.op("scalar", lambda e: e.memzero(bca_sb[:, 0:4]), "s_ac")
        mean_m = b.op("scalar", lambda e: e.activation(
            r_mean[:, :], PF[2][0:1, :], AFCopy, scale=1.0 / D), pr=(2,))
        b.op("scalar", lambda e: e.activation(r_sq[:, :], PF[3][0:1, :], AFCopy, scale=1.0 / D), pr=(3,))
        b.wait("scalar", "s_ac", mean_m)
        if PH2_RCUT < 2:
            return b.op("scalar", lambda e: e.memzero(bca_sb[:, 0:4]), "s_ac")
        m2m = b.op("scalar", lambda e: e.activation(r_m2[:, :], r_mean[:, :], AFSquare), "s_ac")
        if PH2_RCUT < 3:
            return b.op("scalar", lambda e: e.memzero(bca_sb[:, 0:4]), "s_ac")
        b.wait("vector", "s_ac", m2m)
        b.op("vector", lambda e: e.tensor_sub(r_var[:, :], r_sq[:, :], r_m2[:, :]))
        b.wait("vector", "s_ve", b.cnt["s_ve"])
        vv = b.op("vector", lambda e: e.tensor_scalar(
            r_lnv[:, :], r_var[:, :], 1e-5, None, ALU.add), "s_ve")
        if PH2_RCUT < 4:
            b.wait("scalar", "s_ve", vv)
            return b.op("scalar", lambda e: e.memzero(bca_sb[:, 0:4]), "s_ac")
        b.wait("scalar", "s_ve", vv)
        lnv_m = b.op("scalar", lambda e: e.activation(r_var[:, :], r_lnv[:, :], AFLn))
        b.wait("scalar", "s_ac", lnv_m)
        rst = b.op("scalar", lambda e: e.activation(
            r_rstd[:, :], r_var[:, :], AFExp, scale=-0.5), "s_ac")
        if PH2_RCUT < 5:
            return rst
        b.wait("vector", "s_ac", rst)
        rbm = b.op("vector", lambda e: e.tensor_mul(r_b[:, :], r_mean[:, :], r_rstd[:, :]), "s_ve")
        if PH2_RCUT < 6:
            b.wait("scalar", "s_ve", rbm)
            return b.op("scalar", lambda e: e.memzero(bca_sb[:, 0:4]), "s_ac")
        b.wait("tensor", "s_ac", rst)
        b.wait("tensor", "s_ve", rbm)
        b.op("tensor", lambda e: e.matmul(
            PF[4][:, :], onesf_sb[0:1, :], r_rstd[:, :], start=True, stop=True,
            skip_group_check=True), None, pw=(4,))
        bcb = b.op("tensor", lambda e: e.matmul(
            PF[5][:, :], nonesf_sb[0:1, :], r_b[:, :], start=True, stop=True,
            skip_group_check=True), "s_pe", pw=(5,))
        if PH2_RCUT < 7:
            b.wait("scalar", "s_pe", bcb)
            return b.op("scalar", lambda e: e.memzero(bca_sb[:, 0:4]), "s_ac")
        b.wait("scalar", "s_pe", bcb)
        b.op("scalar", lambda e: e.activation(bca_sb[:, :], PF[4][:, :], AFCopy), pr=(4,))
        ev5 = b.op("scalar", lambda e: e.activation(bcb_sb[:, :], PF[5][:, :], AFCopy), "s_ac", pr=(5,))
        return ev5

    xm_dma32 = {}
    xm_dma16 = {}
    ph2_guard = {}
    prev_sc_pe = 0
    prev_sc_ve = 0
    for sc in range(PH2_NSC):
        b.wait("sync", "s_pe", prev_sc_pe)
        b.wait("sync", "s_ve", prev_sc_ve)
        czt = None
        for hh in range(NH):
            czt = b.dma("sync", zt_sb[:, hh * 512:(hh + 1) * 512],
                        z_scr[hh * P:(hh + 1) * P, sc * SCW:(sc + 1) * SCW], "p2l")
        cxq = b.dma("sync", xq_sb[:, :].rearrange("p (j s) -> p j s", s=SCW),
                    x_q[:, sc * SCW:(sc + 1) * SCW].rearrange("(j p) s -> p j s", p=P), "p2l")
        b.wait("tensor", "p2l", cxq)
        b.wait("tensor", "s_ve", prev_sc_ve)
        stat_mm = None
        for dc in range(DC):
            bank = dc % 2
            engn, mi = ph2_guard.get(bank, (None, 0))
            if engn:
                b.wait("tensor", engn, mi)
            mm = None
            for hh in range(NH):
                mm = b.op("tensor", lambda e, bank=bank, hh=hh, dc=dc: e.matmul(
                    PF[bank][:, :], wo_sb[:, hh * D + dc * E: hh * D + (dc + 1) * E],
                    zt_sb[:, hh * 512:(hh + 1) * 512],
                    start=(hh == 0), stop=(hh == NH - 1), skip_group_check=True),
                    "s_pe" if hh == NH - 1 else None, pw=(bank,))
            b.wait("vector", "s_pe", mm)
            ar = b.op("vector", lambda e, dc=dc, bank=bank: e.tensor_add(
                ares_sb[:, dc * 512:(dc + 1) * 512], PF[bank][:, :],
                xq_sb[:, dc * 512:(dc + 1) * 512]), "s_ve", pr=(bank,))
            ph2_guard[bank] = ("s_ve", ar)
            if PH2_PARTS == "ores":
                continue
            b.wait("vector", "s_ve", ar)
            cb = b.op("vector", lambda e, dc=dc: e.tensor_copy(
                ab16_sb[:, :], ares_sb[:, dc * 512:(dc + 1) * 512]), "s_ve")
            b.wait("scalar", "s_ve", cb)
            sq = b.op("scalar", lambda e, dc=dc: e.activation(
                asq_sb[:, :], ares_sb[:, dc * 512:(dc + 1) * 512], AFSquare), "s_ac")
            b.wait("tensor", "s_ve", cb)
            b.wait("tensor", "s_ac", sq)
            b.op("tensor", lambda e, dc=dc: e.matmul(
                PF[2][0:1, :], onesb_sb[:, 0:1], ab16_sb[:, :],
                start=(dc == 0), stop=(dc == DC - 1), skip_group_check=True), None, pw=(2,))
            stat_mm = b.op("tensor", lambda e, dc=dc: e.matmul(
                PF[3][0:1, :], onesb_sb[:, 0:1], asq_sb[:, :],
                start=(dc == 0), stop=(dc == DC - 1), skip_group_check=True),
                "s_pe", pw=(3,))
            if dc < DC - 1:
                b.wait("vector", "s_pe", stat_mm)
                b.wait("scalar", "s_pe", stat_mm)
        if PH2_PARTS in ("ores", "stats"):
            prev_sc_pe = b.cnt["s_pe"]
            prev_sc_ve = b.cnt["s_ve"]
            continue
        bev = ln_rows(stat_mm)
        if PH2_PARTS == "rows":
            prev_sc_pe = b.cnt["s_pe"]
            prev_sc_ve = b.cnt["s_ve"]
            continue
        b.wait("vector", "s_ac", bev)
        for dc in range(DC):
            gi = sc * DC + dc
            xb = dc % 2
            b.wait("vector", f"xm{xb}", xm_dma32.get(gi - 2, 0))
            t1 = b.op("vector", lambda e, dc=dc, xb=xb: e.tensor_mul(
                xmt_sb[xb][:, :], bca_sb[:, :], ares_sb[:, dc * 512:(dc + 1) * 512]), "s_ve")
            b.wait("vector", "s_ve", t1)
            t2 = b.op("vector", lambda e, xb=xb: e.tensor_add(
                xm32_sb[xb][:, :], bcb_sb[:, :], xmt_sb[xb][:, :]), "s_ve")
            if PH2_PARTS == "normnodma":
                continue
            b.wait("scalar", "s_ve", t2)
            b.wait("scalar", f"xm{xb}", xm_dma16.get(gi - 2, 0))
            c16 = b.op("scalar", lambda e, xb=xb: e.activation(
                xm16_sb[xb][:, :], xm32_sb[xb][:, :], AFCopy), "s_ac")
            if PH2_PARTS == "norm16":
                continue
            b.wait("sync", "s_ve", t2)
            xm_dma32[gi] = b.dma(
                "sync", xm32_scr[dc * P:(dc + 1) * P, sc * SCW:(sc + 1) * SCW],
                xm32_sb[xb][:, :], f"xm{xb}")
            b.wait("sync", "s_ac", c16)
            xm_dma16[gi] = b.dma(
                "sync", xmbf_scr[dc * P:(dc + 1) * P, sc * SCW:(sc + 1) * SCW],
                xm16_sb[xb][:, :], f"xm{xb}")
        prev_sc_pe = b.cnt["s_pe"]
        prev_sc_ve = b.cnt["s_ve"]

    b.wait_all_dma("sync")
    barrier()
    if NPHASES < 4:
        return finish()

    hid_sb = vA(0, 64 * 512, BF16)
    up_sb = [vA(64, DC * 512, BF16), vA(80, DC * 512, BF16)]
    xm16f_sb = vB(0, DC * 512, BF16)
    res_sb = vB(16, DC * 512, F32)
    dn_sb = [vB(48, 4 * 512, BF16), vB(52, 4 * 512, BF16)]
    xmr_sb = [vB(56, 512, F32), vB(60, 512, F32)]
    ffn_sb = [vC(2, 512, F32), vC(4, 512, F32)]
    out_sb = [vC(6, 512, F32), vC(8, 512, F32)]

    up_loaded = [0, 0]
    dn_pe_after = [0, 0]
    ph3_guard = {}
    out_dma = {}
    rv_prev = [0, 0]
    prev_sc_pe3 = 0
    for sc in range(PH3_NSC):
        b.wait("sync", "s_pe", prev_sc_pe3)
        cxm = b.dma("sync", xm16f_sb[:, :].rearrange("p (j s) -> p j s", s=SCW),
                    xmbf_scr[:, sc * SCW:(sc + 1) * SCW].rearrange("(j p) s -> p j s", p=P),
                    "p3x")
        b.wait("tensor", "p3x", cxm)
        for hb_i in range(16):
            ub = hb_i % 2
            b.wait("sync", "s_pe", up_loaded[ub])
            cup = b.dma("sync", up_sb[ub][:, :].rearrange("p (dc h) -> p dc h", h=512),
                        up_scr[hb_i * D:(hb_i + 1) * D, :].rearrange("(dc p) h -> p dc h", p=P),
                        f"up{ub}")
            b.wait("tensor", f"up{ub}", cup)
            for ht in range(4):
                bank = ht % 2
                engn, mi = ph3_guard.get(bank, (None, 0))
                if engn:
                    b.wait("tensor", engn, mi)
                mm = None
                for dc in range(DC):
                    mm = b.op("tensor", lambda e, bank=bank, dc=dc, ht=ht, ub=ub: e.matmul(
                        PF[bank][:, :], up_sb[ub][:, dc * 512 + ht * P: dc * 512 + (ht + 1) * P],
                        xm16f_sb[:, dc * 512:(dc + 1) * 512],
                        start=(dc == 0), stop=(dc == DC - 1), skip_group_check=True),
                        "s_pe" if dc == DC - 1 else None, pw=(bank,))
                htg = hb_i * 4 + ht
                b.wait("scalar", "s_pe", mm)
                hv = b.op("scalar", lambda e, htg=htg, bank=bank: e.activation(
                    hid_sb[:, htg * 512:(htg + 1) * 512], PF[bank][:, :], AFRelu,
                    scale=upsc_sb[:, htg:htg + 1]), "s_ac", pr=(bank,))
                ph3_guard[bank] = ("s_ac", hv)
                last_hv = hv
            up_loaded[ub] = b.cnt["s_pe"]
        b.wait("tensor", "s_ac", last_hv)
        for dblk in (range(PH3_NDBLK) if PH3_PARTS in ("hiddown", "downmm", "downfv", "full") else []):
            mm_last = None
            for htq in range(PH3_NHTQ):
                db = htq % 2
                b.wait("sync", "s_pe", dn_pe_after[db])
                cdn = b.dma(
                    "sync", dn_sb[db][:, :].rearrange("p (t d) -> p t d", d=512),
                    down_scr[dblk * DH + htq * 512: dblk * DH + (htq + 1) * 512, :]
                    .rearrange("(t p) d -> p t d", p=P), f"dn{db}")
                b.wait("tensor", f"dn{db}", cdn)
                for tq in range(4):
                    ht64 = htq * 4 + tq
                    for dc4 in range(4):
                        mm_last = b.op("tensor", lambda e, dc4=dc4, tq=tq, db=db, ht64=ht64:
                                       e.matmul(
                            PF[2 + dc4][:, :],
                            dn_sb[db][:, tq * 512 + dc4 * P: tq * 512 + (dc4 + 1) * P],
                            hid_sb[:, ht64 * 512:(ht64 + 1) * 512],
                            start=(ht64 == 0), stop=(ht64 == PH3_NHTQ * 4 - 1), skip_group_check=True),
                            "s_pe" if (tq == 3 and dc4 == 3) else None,
                            pw=(2 + dc4,))
                dn_pe_after[db] = mm_last
            for dc4 in (range(4) if PH3_PARTS != "downmm" else []):
                dcg = dblk * 4 + dc4
                xb = dc4 % 2
                b.wait("scalar", "s_pe", mm_last)
                b.wait("scalar", "s_ve", rv_prev[xb])
                fv = b.op("scalar", lambda e, dc4=dc4, dcg=dcg, xb=xb: e.activation(
                    ffn_sb[xb][:, :], PF[2 + dc4][:, :], AFCopy,
                    scale=dseff_sb[:, dcg:dcg + 1]), "s_ac", pr=(2 + dc4,))
                fv_sem = "s_ac"
                if PH3_PARTS != "downfv":
                    b.wait("sync", "s_ve", rv_prev[xb])
                    cxr = b.dma("sync", xmr_sb[xb][:, :],
                                xm32_scr[dcg * P:(dcg + 1) * P, sc * SCW:(sc + 1) * SCW], f"xr{xb}")
                    b.wait("vector", fv_sem, fv)
                    b.wait("vector", f"xr{xb}", cxr)
                    rv = b.op("vector", lambda e, dcg=dcg, xb=xb: e.tensor_add(
                        res_sb[:, dcg * 512:(dcg + 1) * 512], ffn_sb[xb][:, :],
                        xmr_sb[xb][:, :]), "s_ve")
                    rv_prev[xb] = rv
                b.wait("tensor", fv_sem, fv)
        if PH3_PARTS != "full":
            continue
        stat_mm = None
        b.wait("vector", "s_ve", b.cnt["s_ve"])
        b.wait("scalar", "s_ve", b.cnt["s_ve"])
        for dc in range(DC):
            cb = b.op("vector", lambda e, dc=dc: e.tensor_copy(
                ab16_sb[:, :], res_sb[:, dc * 512:(dc + 1) * 512]), "s_ve")
            b.wait("scalar", "s_ve", cb)
            sq = b.op("scalar", lambda e, dc=dc: e.activation(
                asq_sb[:, :], res_sb[:, dc * 512:(dc + 1) * 512], AFSquare), "s_ac")
            b.wait("tensor", "s_ve", cb)
            b.wait("tensor", "s_ac", sq)
            b.op("tensor", lambda e, dc=dc: e.matmul(
                PF[2][0:1, :], onesb_sb[:, 0:1], ab16_sb[:, :],
                start=(dc == 0), stop=(dc == DC - 1), skip_group_check=True), None, pw=(2,))
            stat_mm = b.op("tensor", lambda e, dc=dc: e.matmul(
                PF[3][0:1, :], onesb_sb[:, 0:1], asq_sb[:, :],
                start=(dc == 0), stop=(dc == DC - 1), skip_group_check=True),
                "s_pe", pw=(3,))
            if dc < DC - 1:
                b.wait("vector", "s_pe", stat_mm)
                b.wait("scalar", "s_pe", stat_mm)
        bev = ln_rows(stat_mm)
        b.wait("vector", "s_ac", bev)
        for dc in range(DC):
            gi = sc * DC + dc
            xb = dc % 2
            b.wait("vector", f"ot{xb}", out_dma.get(gi - 2, 0))
            t1 = b.op("vector", lambda e, dc=dc, xb=xb: e.tensor_mul(
                ffn_sb[xb][:, :], bca_sb[:, :], res_sb[:, dc * 512:(dc + 1) * 512]), "s_ve")
            b.wait("vector", "s_ve", t1)
            t2 = b.op("vector", lambda e, xb=xb: e.tensor_add(
                out_sb[xb][:, :], bcb_sb[:, :], ffn_sb[xb][:, :]), "s_ve")
            b.wait("sync", "s_ve", t2)
            out_dma[gi] = b.dma(
                "sync", outT[dc * P:(dc + 1) * P, sc * SCW:(sc + 1) * SCW],
                out_sb[xb][:, :], f"ot{xb}")
        b.wait("tensor", "s_ve", b.cnt["s_ve"])
        prev_sc_pe3 = b.cnt["s_pe"]

    return finish()


def kernel(L0_attn_x, L0_attn_W_Q, L0_attn_W_K, L0_attn_W_V, L0_attn_W_O,
           L0_ffn_up_weight, L0_ffn_down_weight, L0_ffn_up_scale,
           L0_ffn_down_scale, L0_ffn_output_scale):
    x = np.asarray(L0_attn_x, dtype=np.float32)
    wq = np.asarray(L0_attn_W_Q, dtype=np.float32)
    wk = np.asarray(L0_attn_W_K, dtype=np.float32)
    wv = np.asarray(L0_attn_W_V, dtype=np.float32)
    wo = np.asarray(L0_attn_W_O, dtype=np.float32)
    upw = np.asarray(L0_ffn_up_weight, dtype=np.float32)
    dnw = np.asarray(L0_ffn_down_weight, dtype=np.float32)
    ups = np.asarray(L0_ffn_up_scale, dtype=np.float32)
    dns = np.asarray(L0_ffn_down_scale, dtype=np.float32)
    osc = np.asarray(L0_ffn_output_scale, dtype=np.float32)

    xT = np.ascontiguousarray(x.transpose(0, 2, 1))
    wq_t = np.ascontiguousarray(wq.transpose(0, 2, 1)).reshape(NH * D, E)
    wk_t = np.ascontiguousarray(wk.transpose(0, 2, 1)).reshape(NH * D, E)
    wv_t = np.ascontiguousarray(wv.transpose(0, 2, 1)).reshape(NH * D, E)
    wo_t = np.ascontiguousarray(wo.transpose(0, 2, 1)).reshape(NH * E, D)
    up_t = np.ascontiguousarray(upw.T)
    down_t = np.ascontiguousarray(dnw.T)
    upsc2d = np.ascontiguousarray(ups.reshape(64, P))
    dssc2d = np.ascontiguousarray(dns.reshape(16, P))
    osc2d = np.ascontiguousarray(osc.reshape(1, 1))

    eye_f32 = np.eye(P, dtype=np.float32)
    eye_bf = eye_f32.astype(ml_dtypes.bfloat16)
    ones_f32 = np.ones((P, P), dtype=np.float32)
    ones_bf = ones_f32.astype(ml_dtypes.bfloat16)
    nones_f32 = -ones_f32

    tri = np.triu(np.full((P, P), -1e9, dtype=np.float32), k=1)
    masks = []
    for par in range(2):
        m = np.zeros((P, 512), dtype=np.float32)
        if par == 0:
            m[:, 256:384] = tri
            m[:, 384:512] = -1e9
        else:
            m[:, 384:512] = tri
        masks.append(m)

    nc = build_program()

    in_maps = []
    for c in range(8):
        bb, par = c // 2, c % 2
        xb_T = xT[bb]
        x_q = np.ascontiguousarray(
            xb_T.reshape(D, 32, P)[:, par::2, :].reshape(D, SOWN))
        in_maps.append({
            "x_kv": np.ascontiguousarray(xb_T),
            "x_q": x_q,
            "wq_t": wq_t, "wk_t": wk_t, "wv_t": wv_t, "wo_t": wo_t,
            "up_t": up_t, "down_t": down_t,
            "upsc2d": upsc2d, "dssc2d": dssc2d, "osc": osc2d,
            "mask512": masks[par],
            "eye_bf": eye_bf, "eye_f32": eye_f32,
            "ones_bf": ones_bf, "ones_f32": ones_f32, "nones_f32": nones_f32,
        })

    res = run_bass_kernel_spmd(nc, in_maps, core_ids=list(range(8)))
    global LAST_RES
    LAST_RES = res

    out = np.empty((4, S, D), dtype=np.float32)
    for bb in range(4):
        full = np.empty((D, 32, P), dtype=np.float32)
        for par in range(2):
            shard = res.results[2 * bb + par]["outT"]
            full[:, par::2, :] = shard.reshape(D, NSLOT, P)
        out[bb] = full.reshape(D, S).T
    return out
